# version 09
# baseline (speedup 1.0000x reference)
import contextlib
import numpy as np
import ml_dtypes
import concourse.bass as bass
import concourse.mybir as mybir
from concourse.bass_utils import run_bass_kernel_spmd

F32 = mybir.dt.float32
BF16 = mybir.dt.bfloat16
AF = mybir.ActivationFunctionType
ALU = mybir.AluOpType
AX = mybir.AxisListType

_T1, _T2, _T3, _T4 = 0.0625, 0.5625, 1.5625, 3.0625
_T5, _T6, _T7 = 6.25, 12.25, 25.0

_OPS = {}


def _register_fp4_ops():
    if _OPS:
        return _OPS["S1"], _OPS["S2"]
    from concourse.dve_op import DveOp
    import concourse.dve_ops as dve_ops
    from concourse.dve_spec import (
        Spec, Src0, Src1, C0, C1, C2, C3, lower, _spill_c3_to_src1)
    from concourse.dve_table_gen import dve_ver_for

    def _mk(name, spec, ref, row):
        ver = dve_ver_for("TRN2")
        sha = lower(spec, ver=ver, perf_en=False, name=name).sha(ver)
        op = DveOp(name, spec, subdim=False, uops_sha={ver: sha}, reference=ref)
        if all(o.name != name for o in dve_ops.OPS):
            dve_ops.OPS.append(op)
            dve_ops._SUB_OPCODE_FOR_NAME[name] = row
            dve_ops.CUSTOM_DVE_SPECS[name] = spec
            dve_ops.CUSTOM_DVE_REFS[name] = ref
        return op

    w = Src0
    body1 = _spill_c3_to_src1(((w > C0) + (w > C1)) + ((w > C2) + (w > C3)))

    def ref1(in0, in1, s0, s1, imm2):
        v = in0
        r = ((v > s0).astype(in0.dtype) + (v > s1) + (v > imm2) + (v > in1))
        return r.astype(in0.dtype)

    c3_ = w > C2
    g = ((w > C0) + (w > C1)) + (c3_ + c3_)
    body2 = (g + g) + Src1

    def ref2(in0, in1, s0, s1, imm2):
        v = in0
        gg = (v > s0).astype(in0.dtype) + (v > s1) + 2.0 * (v > imm2)
        return (2.0 * gg + in1).astype(in0.dtype)

    _OPS["S1"] = _mk("ANT_FP4_S1", Spec(body=body1), ref1, 30)
    _OPS["S2"] = _mk("ANT_FP4_S2", Spec(body=body2), ref2, 31)
    return _OPS["S1"], _OPS["S2"]


class Sem:
    def __init__(self, h):
        self.h = h
        self.v = 0

    def inc(self, n=1):
        self.v += n
        return self.h


def build_core_program(S=4096, debug=False, nonce=0):
    op1, op2 = _register_fp4_ops()
    D, I, H, LH, LD = 1024, 2816, 16, 8, 512
    NL = S // 64
    NSB = S // 1024
    NG = S // 128
    KT = D // 128
    MT = I // 128
    LT = LD // 128

    nc = bass.Bass("TRN2", target_bir_lowering=False)

    x_h = nc.dram_tensor("x", [S, D], F32, kind="ExternalInput").ap()
    if nonce:
        nc.dram_tensor("nonce_in", [1, 64 + nonce], F32, kind="ExternalInput").ap()
    wq_h = nc.dram_tensor("wq", [D, D], F32, kind="ExternalInput").ap()
    wk_h = nc.dram_tensor("wk", [D, D], F32, kind="ExternalInput").ap()
    wv_h = nc.dram_tensor("wv", [D, D], F32, kind="ExternalInput").ap()
    wgq_h = nc.dram_tensor("wgq", [D, LD], F32, kind="ExternalInput").ap()
    wgk_h = nc.dram_tensor("wgk", [D, LD], F32, kind="ExternalInput").ap()
    wgv_h = nc.dram_tensor("wgv", [D, LD], F32, kind="ExternalInput").ap()
    rout_h = nc.dram_tensor("r_out_w", [LD, D], F32, kind="ExternalInput").ap()
    glw_h = nc.dram_tensor("glw", [H, 64], F32, kind="ExternalInput").ap()
    grw_h = nc.dram_tensor("grw", [H, 64], F32, kind="ExternalInput").ap()
    wout_h = nc.dram_tensor("w_out", [D, D], F32, kind="ExternalInput").ap()
    wg_h = nc.dram_tensor("w_gate", [I, D], F32, kind="ExternalInput").ap()
    wu_h = nc.dram_tensor("w_up", [I, D], F32, kind="ExternalInput").ap()
    wd_h = nc.dram_tensor("w_down", [D, I], F32, kind="ExternalInput").ap()
    out_h = nc.dram_tensor("out", [S, D], F32, kind="ExternalOutput").ap()
    _scr = "ExternalOutput" if debug else "Internal"
    xq_h = nc.dram_tensor("xq_scr", [S, D], BF16, kind=_scr).ap()
    mixed_h = nc.dram_tensor("mixed_scr", [S, D], BF16, kind=_scr).ap()
    wgT_h = nc.dram_tensor("wgT_scr", [KT, 128, I], BF16, kind=_scr).ap()
    wuT_h = nc.dram_tensor("wuT_scr", [KT, 128, I], BF16, kind="Internal").ap()
    wdT_h = nc.dram_tensor("wdT_scr", [MT, 128, D], BF16, kind="Internal").ap()

    es = contextlib.ExitStack()

    def sb(name, shape, dt):
        return es.enter_context(nc.sbuf_tensor(name, shape, dt))

    qx2 = sb("qx2", [128, 1, D], F32)
    qsgn = sb("qsgn", [128, 2, D], BF16)
    qw = sb("qw", [128, 1, D], F32)
    qA = sb("qA", [128, 1, D], BF16)
    qH = sb("qH", [128, 1, D], BF16)
    qM = sb("qM", [128, 1, D], BF16)
    qam = sb("qam", [128, 2, 64], F32)
    qs6 = sb("qs6", [128, 2, 64], F32)
    qri = sb("qri", [128, 2, 64], F32)
    qri2 = sb("qri2", [128, 2, 64], F32)
    qhm = sb("qhm", [128, 2, 64], BF16)
    c3col = sb("c3col", [128, 1], F32)

    psA = es.enter_context(nc.psum_tensor("psA", [128, 1024], F32))
    psB = es.enter_context(nc.psum_tensor("psB", [128, 1024], F32))
    psC = es.enter_context(nc.psum_tensor("psC", [128, 1024], F32))
    psD = es.enter_context(nc.psum_tensor("psD", [128, 1024], F32))

    sems = {n: Sem(es.enter_context(nc.semaphore(name="sem_" + n)))
            for n in ["qd", "gd", "tp", "st", "v", "s", "t", "g", "w0", "w1"]}
    QD, GD, TP, ST = sems["qd"], sems["gd"], sems["tp"], sems["st"]
    SV, SS, TT, GG = sems["v"], sems["s"], sems["t"], sems["g"]
    WP = [sems["w0"], sems["w1"]]

    es1 = contextlib.ExitStack()
    def sb1(name, shape, dt):
        return es1.enter_context(nc.sbuf_tensor(name, shape, dt))
    stat = sb1("stat", [128, 2, 12], F32)
    aggr = sb1("aggr", [128, 2, 2], F32)
    lncol = sb1("lncol", [128, 8], F32)
    gpad = sb1("gpad", [128, 8, 32], BF16)
    glw_sb = sb1("glw_sb", [16, 2, 64], F32)
    glw_T = sb1("glw_T", [64, 2 * H], BF16)
    mask_loc = sb1("mask_loc", [128, 384], BF16)
    pcol = sb1("pcol", [128, 1], F32)
    latthr = sb1("latthr", [64, NG], F32)
    latnegq = sb1("latnegq", [64, 128], F32)
    fcol = sb1("fcol", [128, 4], F32)
    idmat = sb1("idmat", [128, 128], F32)
    routw = sb1("routw", [128, LT, D], BF16)
    xin = sb1("xin", [128, 2, D], F32)
    xn_sb = sb1("xn_sb", [128, 2, D], BF16)
    xq_sb = sb1("xq_sb", [128, 2, D], BF16)
    xn_fm = sb1("xn_fm", [128, KT, 1024], BF16)
    wpan = sb1("wpan", [128, 2, KT, 128], BF16)
    qbuf = sb1("qbuf", [128, KT, 1536], BF16)
    kbuf = sb1("kbuf", [128, KT, 1536], BF16)
    vtmp = sb1("vtmp", [128, D], BF16)
    vtsc = sb1("vtsc", [128, 8, 128], BF16)
    v65 = sb1("v65", [128, 12, H * 65], BF16)
    ebuf = sb1("ebuf", [128, 8, 384], BF16)
    gqbuf = sb1("gqbuf", [128, LT, 1280], BF16)
    glcb = sb1("glcb", [128, 2, 10, H], BF16)
    glps = qsgn[0:16, :, :]
    glcols = sb1("glcols", [128, 10, H], F32)
    grcols = sb1("grcols", [128, 10, H], F32)
    pooled = sb1("pooled", [128, KT, 16], BF16)
    pooledf = sb1("pooledf", [128, KT, 16], F32)
    gkfm = sb1("gkfm", [128, LT, NL], BF16)
    gv65 = sb1("gv65", [64, LH, 65], BF16)
    gvfm = sb1("gvfm", [128, LT, NL], F32)
    mixed_tm = sb1("mixed_tm", [128, 10, D], BF16)
    remp_tm = sb1("remp_tm", [128, 10, LD], BF16)
    remp_fm = sb1("remp_fm", [128, LT, 128], BF16)
    remg = sb1("remg", [128, D], BF16)
    fw32 = sb1("fw32", [128, 1, D], F32)
    fwq = sb1("fwq", [128, 1, D], BF16)
    fwT = sb1("fwT", [128, KT, 128], BF16)
    lexps = sb1("lexps", [64, 2, 128], BF16)

    fidx = qx2[:, 0, 0:384]
    mtmp = qx2[:, 0, 384:768]

    block = es.enter_context(nc.Block(no_gpsimd_drain=True))

    def on(name, fn):
        getattr(block, name)(fn)

    def _g(g):
        g.iota(fidx, pattern=[[1, 384]], base=0, channel_multiplier=0,
               allow_small_or_imprecise_dtypes=True)
        g.iota(pcol[:], pattern=[[0, 1]], base=0, channel_multiplier=1,
               allow_small_or_imprecise_dtypes=True)
        g.iota(latthr[:], pattern=[[128, NG]], base=1, channel_multiplier=-64,
               allow_small_or_imprecise_dtypes=True)
        g.iota(latnegq[:], pattern=[[-1, 128]], base=0, channel_multiplier=0,
               allow_small_or_imprecise_dtypes=True).then_inc(GG.inc(), 1)
        for k in range(LT):
            g.dma_start(out=routw[:, k, :], in_=rout_h[k * 128:(k + 1) * 128, :]
                        ).then_inc(GD.inc(16), 16)
    on("gpsimd", _g)

    def _gw(g):
        g.dma_start(out=glw_sb[:, 0, :], in_=glw_h[:]).then_inc(GD.inc(16), 16)
        g.wait_ge(GD.h, GD.v)
        g.dma_start(out=glw_sb[:, 1, :], in_=grw_h[:]).then_inc(GD.inc(16), 16)
    on("gpsimd", _gw)
    gd_glw = GD.v

    def _vid(v, gd_glw=gd_glw):
        v.wait_ge(GG.h, GG.v)
        v.scalar_tensor_tensor(out=idmat[:], in0=fidx[:, 0:128][0:128, 0:128],
                               scalar=0.0, in1=pcol[:].broadcast_to([128, 128]),
                               op0=ALU.add, op1=ALU.is_equal)
        v.drain().then_inc(SV.inc(), 1)
    on("vector", _vid)
    sv_id = SV.v

    def _tgw(t, gd_glw=gd_glw, sv_id=sv_id):
        t.wait_ge(GD.h, gd_glw)
        t.wait_ge(SV.h, sv_id)
        t.transpose(psA[0:64, 0:16], glw_sb[:, 0, :], idmat[0:16, 0:16])
        _li = t.transpose(psB[0:64, 0:16], glw_sb[:, 1, :], idmat[0:16, 0:16])
        _li.then_inc(TT.inc(), 1)
    on("tensor", _tgw)
    tt_glw = TT.v

    def _sgw(s, tt_glw=tt_glw):
        s.wait_ge(TT.h, tt_glw)
        s.activation(out=glw_T[:, 0:16], in_=psA[0:64, 0:16], func=AF.Identity)
        _li = s.activation(out=glw_T[:, 16:32], in_=psB[0:64, 0:16],
                           func=AF.Identity)
        _li.then_inc(SS.inc(), 1)
    on("scalar", _sgw)
    ss_glw = SS.v

    def _v(v):
        v.memset(c3col[:], _T4)
        v.memset(v65[:], 0.0)
        v.memset(gkfm[:], 0.0)
        v.memset(gvfm[:], 0.0)
        v.memset(gv65[:], 0.0)
        v.drain()
        v.memset(v65[:].rearrange("p a (h c) -> p a h c", c=65)[:, :, :, 64:65], 1.0)
        v.memset(gv65[:, :, 64:65], 1.0)
        v.wait_ge(GG.h, GG.v)
        v.scalar_tensor_tensor(out=mtmp, in0=fidx, scalar=0.0,
                               in1=pcol[:].broadcast_to([128, 384]),
                               op0=ALU.add, op1=ALU.is_ge)
        v.drain()
        v.scalar_tensor_tensor(out=fidx, in0=fidx, scalar=-256.0,
                               in1=pcol[:].broadcast_to([128, 384]),
                               op0=ALU.add, op1=ALU.is_lt)
        v.drain()
        v.tensor_tensor(out=mask_loc[:], in0=mtmp, in1=fidx, op=ALU.mult)
        v.wait_ge(SS.h, ss_glw)
        v.memset(gpad[:], 0.0)
        v.drain()
        gpf = gpad[:].rearrange("p a b -> p (a b)")
        for j in range(2):
            v.tensor_copy(out=gpf[64 * j:64 * (j + 1), j:j + 34 * 7 + 1:34],
                          in_=glw_T[:, j:16:2])
            v.tensor_copy(out=gpf[64 * j:64 * (j + 1), 16 + j:16 + j + 34 * 7 + 1:34],
                          in_=glw_T[:, 16 + j:32:2])
        v.drain().then_inc(SV.inc(), 1)
    on("vector", _v)
    base_sv = SV.v


    def emit_quant(slot, W, dst_ap, src_ap, dst_waits=None, src_waits=None,
                   lead_drain=False):
        nb = W // 16
        x2 = qx2[:, 0, 0:W]
        sg = qsgn[:, slot, 0:W]
        wv = qw[:, 0, 0:W]
        Av = qA[:, 0, 0:W]
        Hv = qH[:, 0, 0:W]
        Mv = qM[:, 0, 0:W]
        am = qam[:, slot, 0:nb]
        s6 = qs6[:, slot, 0:nb]
        ri = qri[:, slot, 0:nb]
        ri2 = qri2[:, slot, 0:nb]
        hm = qhm[:, slot, 0:nb]

        def _v1(v):
            if src_waits:
                for sh, sv_ in src_waits:
                    v.wait_ge(sh, sv_)
            if lead_drain:
                v.drain()
            v.tensor_reduce(out=am, in_=src_ap.rearrange("p (nb k) -> p nb k", k=16),
                            axis=AX.X, op=ALU.max, apply_absolute_value=True)
            v.tensor_scalar(out=sg, in0=src_ap, scalar1=0.0, scalar2=2.0,
                            op0=ALU.is_gt, op1=ALU.mult)
            v.drain()
            v.tensor_scalar(out=s6, in0=am, scalar1=1.0 / 6.0, scalar2=1e-12,
                            op0=ALU.mult, op1=ALU.max)
            v.tensor_scalar(out=hm, in0=am, scalar1=1.0 / 12.0, scalar2=5e-13,
                            op0=ALU.mult, op1=ALU.max)
            v.drain()
            _li = v.reciprocal(out=ri, in_=s6)
            _li.then_inc(SV.inc(), 1)
        on("vector", _v1)

        def _gq(g):
            g.wait_ge(SV.h, SV.v)
            g.tensor_tensor(out=ri2, in0=ri, in1=ri, op=ALU.mult)
            g.tensor_tensor(out=x2, in0=src_ap, in1=src_ap, op=ALU.mult)
            g.drain()
            r2b = ri2.unsqueeze(2).broadcast_to([128, nb, 16])
            g.tensor_tensor(out=wv.rearrange("p (nb k) -> p nb k", k=16),
                            in0=x2.rearrange("p (nb k) -> p nb k", k=16),
                            in1=r2b, op=ALU.mult).then_inc(GG.inc(), 1)
        on("gpsimd", _gq)

        def _v3(v):
            v.wait_ge(GG.h, GG.v)
            if dst_waits:
                for sh, sv_ in dst_waits:
                    v.wait_ge(sh, sv_)
            v._custom_dve(op1, out=Av, in0=wv, in1=c3col[:],
                          s0=_T1, s1=_T2, imm2=_T3)
            v.drain()
            v._custom_dve(op2, out=Hv, in0=wv, in1=Av, s0=_T5, s1=_T6, imm2=_T7)
            hm_b = hm.unsqueeze(2).broadcast_to([128, nb, 16])
            v.scalar_tensor_tensor(out=Mv.rearrange("p (nb k) -> p nb k", k=16),
                                   in0=sg.rearrange("p (nb k) -> p nb k", k=16),
                                   scalar=-1.0, in1=hm_b, op0=ALU.add, op1=ALU.mult)
            v.drain()
            _li = v.tensor_tensor(out=dst_ap, in0=Hv, in1=Mv, op=ALU.mult)
            _li.then_inc(SV.inc(), 1)
        on("vector", _v3)

    ffn_tiles = ([("g", t, 0) for t in range(MT)] +
                 [("u", t, 0) for t in range(MT)] +
                 [("d", t, c) for t in range(KT) for c in range(3)])

    def emit_ffn_wtile(kind, t, c, slot):
        if kind == "g":
            src, dsth, W, c0 = wg_h[t * 128:(t + 1) * 128, :], wgT_h, D, 0
        elif kind == "u":
            src, dsth, W, c0 = wu_h[t * 128:(t + 1) * 128, :], wuT_h, D, 0
        else:
            c0 = c * 1024
            W = min(1024, I - c0)
            src = wd_h[t * 128:(t + 1) * 128, c0:c0 + W]
            dsth = wdT_h
        nch = W // 128

        def _sy(sy):
            sy.wait_ge(SV.h, ffn_guard["sv"])
            sy.wait_ge(SS.h, ffn_guard["ss"])
            sy.dma_start(out=fw32[:, slot, 0:W], in_=src).then_inc(QD.inc(16), 16)
        on("sync", _sy)
        qd_t = QD.v

        emit_quant(slot, W, fwq[:, slot, 0:W], src_ap=fw32[:, slot, 0:W],
                   dst_waits=[(TP.h, TP.v)], src_waits=[(QD.h, qd_t)])
        sv_q = SV.v
        ffn_guard["sv"] = SV.v
        ffn_guard["ss"] = SS.v

        def _sy2(sy, sv_q=sv_q, nch=nch, kind=kind, t=t, c0=c0, dsth=dsth, W=W,
                 slot=slot, stw=None):
            sy.wait_ge(SV.h, sv_q)
            sy.wait_ge(ST.h, ST.v)
            sy.dma_start_transpose(out=fwT[:, 0:nch, :], in_=fwq[:, slot, 0:W]
                                   ).then_inc(TP.inc(16), 16)
            sy.wait_ge(TP.h, TP.v)
            dst = dsth.rearrange("c p n -> p c n")[:, c0 // 128:c0 // 128 + nch,
                                                   t * 128:(t + 1) * 128]
            sy.dma_start(out=dst, in_=fwT[:, 0:nch, :]).then_inc(ST.inc(16), 16)
        on("sync", _sy2)
        ffn_guard["tp"] = TP.v

    sc_ps = [psC[:, 0:384], psC[:, 512:896]]
    o65_ps = [psD[:, 0:65], psD[:, 512:577]]
    NLP = min(NL, 64)
    lsc_ps = [psB[0:NLP, 0:128], psB[0:NLP, 512:640]]
    r65_ps = [psA[:, 0:65], psA[:, 512:577]]

    prev_tp_xnsb = [0]
    prev_st_xqsb = [0]
    prev_st_mixed = [0]
    ffn_guard = {"sv": 0, "ss": 0, "tp": 0}
    xin_free = {0: 0, 1: 0}
    xnsb_free = {0: 0, 1: 0}
    xq_free = {0: 0, 1: 0}
    for sbi in range(NSB):
        t0 = sbi * 1024
        last = (sbi == NSB - 1)
        org = 8 * sbi - 4
        gorg = 8 * sbi - 2

        tpa = {}
        sta = {}
        SA0 = SS.v
        sv_var = {}
        for st in range(8):
            slot = st % 2
            tok = t0 + st * 128

            def _sy(sy, slot=slot, tok=tok):
                sy.wait_ge(SV.h, xin_free[slot])
                sy.dma_start(out=xin[:, slot, :], in_=x_h[tok:tok + 128, :]
                             ).then_inc(QD.inc(16), 16)
            on("sync", _sy)
            qd_tile = QD.v

            def _v(v, slot=slot, qd=qd_tile):
                v.wait_ge(QD.h, qd)
                xi = xin[:, slot, :]
                v.bn_stats(out=stat[:, slot, 0:6], in_=xi[:, 0:512])
                v.bn_stats(out=stat[:, slot, 6:12], in_=xi[:, 512:1024])
                v.drain()
                v.bn_aggr(out=aggr[:, slot, :], in_=stat[:, slot, :])
                v.drain()
                _li = v.tensor_scalar(out=lncol[:, 4 * slot:4 * slot + 1],
                                      in0=aggr[:, slot, 1:2], scalar1=1e-5,
                                      scalar2=None, op0=ALU.add)
                _li.then_inc(SV.inc(), 1)
            on("vector", _v)
            sv_var[st] = SV.v

            rstd = lncol[:, 4 * slot + 2:4 * slot + 3]

            def _v2b(v, slot=slot, rstd=rstd, ggw=GG.v, st=st, SA0=SA0):
                v.wait_ge(SS.h, SA0 + st + 1)
                v.wait_ge(GG.h, ggw)
                v.reciprocal(out=rstd, in_=lncol[:, 4 * slot + 1:4 * slot + 2])
                v.drain()
                _li = v.scalar_tensor_tensor(out=qw[:, 0, :], in0=xin[:, slot, :],
                                             scalar=aggr[:, slot, 0:1],
                                             in1=rstd.broadcast_to([128, D]),
                                             op0=ALU.subtract, op1=ALU.mult)
                _li.then_inc(SV.inc(), 1)
            on("vector", _v2b)
            sv_xn = SV.v
            xin_free[slot] = SV.v

            def _gxn(g, slot=slot, sv_xn=sv_xn, tp_guard=xnsb_free[slot]):
                g.wait_ge(SV.h, sv_xn)
                g.wait_ge(TP.h, tp_guard)
                g.tensor_copy(out=xn_sb[:, slot, :], in_=qw[:, 0, :]
                              ).then_inc(GG.inc(), 1)
            on("gpsimd", _gxn)
            gg_xn = GG.v

            def _sy2(sy, st=st, slot=slot, gg_xn=gg_xn):
                sy.wait_ge(GG.h, gg_xn)
                sy.wait_ge(TP.h, xnsb_free[1 - slot])
                sy.dma_start_transpose(out=xn_fm[:, :, st * 128:(st + 1) * 128],
                                       in_=xn_sb[:, slot, :]).then_inc(TP.inc(16), 16)
            on("sync", _sy2)
            tpa[st] = TP.v
            xnsb_free[slot] = TP.v

            emit_quant(slot, D, xq_sb[:, slot, :], src_ap=qw[:, 0, :],
                       dst_waits=[(ST.h, ST.v)], lead_drain=True)
            sv_xq = SV.v

            def _sy3(sy, tok=tok, slot=slot, sv_xq=sv_xq):
                sy.wait_ge(SV.h, sv_xq)
                sy.dma_start(out=xq_h[tok:tok + 128, :], in_=xq_sb[:, slot, :]
                             ).then_inc(ST.inc(16), 16)
            on("sync", _sy3)
            sta[st] = ST.v
            xq_free[slot] = ST.v

        def _sqA(s, waits=dict(sv_var), SA0=SA0):
            for st in range(8):
                slot = st % 2
                s.wait_ge(SV.h, waits[st])
                s.activation(out=lncol[:, 4 * slot + 1:4 * slot + 2],
                             in_=lncol[:, 4 * slot:4 * slot + 1],
                             func=AF.Sqrt).then_inc(SS.inc(), 1)
        on("scalar", _sqA)
        assert SS.v == SA0 + 8

        tp_xn = TP.v
        prev_tp_xnsb[0] = tpa[7]
        prev_st_xqsb[0] = sta[7]

        n_all = len(ffn_tiles)
        per_sb = (n_all + NSB - 1) // NSB
        for wi in range(per_sb):
            idx = sbi * per_sb + wi
            if idx < n_all:
                kind, t, c = ffn_tiles[idx]
                emit_ffn_wtile(kind, t, c, 0)

        panels = [(p, d) for p in range(3) for d in range(KT)] + \
                 [(3, d) for d in range(LT)]
        tt_done = {}
        SSB0 = SS.v
        TTB0 = TT.v
        vtp_rec = {}
        for pi, (proj, dt_) in enumerate(panels):
            slot = pi % 2
            srcw = [wq_h, wk_h, wv_h, wgq_h][proj]

            wp = WP[slot]

            def _g(g, slot=slot, srcw=srcw, dt_=dt_, pi=pi, wp=wp):
                if pi >= 2:
                    g.wait_ge(TT.h, tt_done[pi - 2])
                g.dma_start(out=wpan[:, slot, :, :],
                            in_=srcw[:, dt_ * 128:(dt_ + 1) * 128]
                            .rearrange("(c p) n -> p c n", p=128)
                            ).then_inc(wp.inc(16), 16)
            on("gpsimd", _g)
            gd_p = wp.v
            ps = psA if pi % 2 == 0 else psB
            evac_prev = SSB0 + pi - 1 if pi >= 2 else SSB0

            wph = wp.h

            def _t(t, slot=slot, gd_p=gd_p, ps=ps, tp_xn=tp_xn, evac_prev=evac_prev,
                   wph=wph):
                t.wait_ge(wph, gd_p)
                t.wait_ge(TP.h, tp_xn)
                t.wait_ge(SS.h, evac_prev)
                for k in range(KT):
                    t.matmul(ps[:, 0:512], wpan[:, slot, k, :], xn_fm[:, k, 0:512],
                             start=(k == 0), stop=(k == KT - 1))
                    _li = t.matmul(ps[:, 512:1024], wpan[:, slot, k, :],
                             xn_fm[:, k, 512:1024],
                             start=(k == 0), stop=(k == KT - 1))
                _li.then_inc(TT.inc(), 1)
            on("tensor", _t)
            tt_done[pi] = TT.v
            vtp_rec[pi] = TP.v

            if proj == 2:
                def _sy(sy, dt_=dt_, ss_p=SSB0 + pi + 1, svw=SV.v):
                    sy.wait_ge(SS.h, ss_p)
                    sy.wait_ge(SV.h, svw)
                    sy.dma_start_transpose(out=vtsc[:], in_=vtmp[:]
                                           ).then_inc(TP.inc(16), 16)
                on("sync", _sy)

                def _vv(v, dt_=dt_, tpw=TP.v):
                    v.wait_ge(TP.h, tpw)
                    for j in range(2):
                        h = 2 * dt_ + j
                        v.tensor_copy(out=v65[:, 4:12, 65 * h:65 * h + 64],
                                      in_=vtsc[:, :, 64 * j:64 * (j + 1)])
                    v.drain().then_inc(SV.inc(), 1)
                on("vector", _vv)

        def _sbev(s, panels=panels, TTB0=TTB0, vt=dict(vtp_rec)):
            for pi, (proj, dt_) in enumerate(panels):
                ps = psA if pi % 2 == 0 else psB
                s.wait_ge(TT.h, TTB0 + pi + 1)
                if proj == 2:
                    s.wait_ge(TP.h, vt[pi])
                if proj == 0:
                    _li = s.activation(out=qbuf[:, dt_, 512:1536], in_=ps[:],
                                       func=AF.Identity)
                elif proj == 1:
                    _li = s.activation(out=kbuf[:, dt_, 512:1536], in_=ps[:],
                                       func=AF.Identity)
                elif proj == 2:
                    _li = s.activation(out=vtmp[:], in_=ps[:], func=AF.Identity)
                else:
                    _li = s.activation(out=gqbuf[:, dt_, 256:1280], in_=ps[:],
                                       func=AF.Identity)
                _li.then_inc(SS.inc(), 1)
        on("scalar", _sbev)
        assert SS.v == SSB0 + len(panels)

        tp_v65 = TP.v

        def _vp(v, tp_v65=tp_v65):
            v.wait_ge(TP.h, tp_v65)
            v.tensor_reduce(out=pooledf[:],
                            in_=xn_fm[:].rearrange("p c (l w) -> p (c l) w", w=64)
                            .rearrange("p (c l) w -> p c l w", c=KT),
                            axis=AX.X, op=ALU.add)
            v.drain()
            v.tensor_scalar(out=pooled[:], in0=pooledf[:], scalar1=1.0 / 64.0,
                            scalar2=None, op0=ALU.mult)
            v.drain().then_inc(SV.inc(), 1)
        on("vector", _vp)
        sv_pool = SV.v
        l0 = sbi * 16

        for gt in range(LT):
            def _g(g, gt=gt):
                g.wait_ge(TT.h, TT.v)
                g.dma_start(out=wpan[:, 0, :, :],
                            in_=wgk_h[:, gt * 128:(gt + 1) * 128]
                            .rearrange("(c p) n -> p c n", p=128)).then_inc(GD.inc(16), 16)
                g.dma_start(out=wpan[:, 1, :, :],
                            in_=wgv_h[:, gt * 128:(gt + 1) * 128]
                            .rearrange("(c p) n -> p c n", p=128)).then_inc(GD.inc(16), 16)
            on("gpsimd", _g)
            gd_g = GD.v

            def _t(t, gt=gt, gd_g=gd_g, sv_pool=sv_pool):
                t.wait_ge(GD.h, gd_g)
                t.wait_ge(SV.h, sv_pool)
                t.wait_ge(SS.h, SS.v)
                for k in range(KT):
                    t.matmul(psA[:, 0:16], wpan[:, 0, k, :], pooled[:, k, :],
                             start=(k == 0), stop=(k == KT - 1))
                    _li = t.matmul(psB[:, 0:16], wpan[:, 1, k, :], pooled[:, k, :],
                             start=(k == 0), stop=(k == KT - 1))
                _li.then_inc(TT.inc(), 1)
            on("tensor", _t)
            tt_g = TT.v

            def _s(s, gt=gt, tt_g=tt_g, l0=l0):
                s.wait_ge(TT.h, tt_g)
                s.activation(out=gkfm[:, gt, l0:l0 + 16], in_=psA[:, 0:16],
                             func=AF.Identity)
                _li = s.activation(out=gvfm[:, gt, l0:l0 + 16], in_=psB[:, 0:16],
                                   func=AF.Identity)
                _li.then_inc(SS.inc(), 1)
            on("scalar", _s)
            ss_g = SS.v

            def _tv(t, gt=gt, ss_g=ss_g):
                t.wait_ge(SS.h, ss_g)
                _li = t.transpose(psC[0:NLP, 0:128], gvfm[:, gt, 0:NL], idmat[:])
                _li.then_inc(TT.inc(), 1)
            on("tensor", _tv)
            tt_gv = TT.v

            def _sv(s, gt=gt, tt_gv=tt_gv):
                s.wait_ge(TT.h, tt_gv)
                s.activation(out=gv65[0:NL, 2 * gt, 0:64],
                             in_=psC[0:NLP, 0:64], func=AF.Identity)
                _li = s.activation(out=gv65[0:NL, 2 * gt + 1, 0:64],
                             in_=psC[0:NLP, 64:128], func=AF.Identity)
                _li.then_inc(SS.inc(), 1)
            on("scalar", _sv)

        def _tf(t):
            t.wait_ge(SS.h, SS.v)
            for f in range(KT):
                t.matmul(psA[0:16, 0:512], gpad[:, f, 0:16],
                         qbuf[:, f, 512:1024], start=(f == 0), stop=(f == KT - 1))
                t.matmul(psA[0:16, 512:1024], gpad[:, f, 0:16],
                         qbuf[:, f, 1024:1536], start=(f == 0), stop=(f == KT - 1))
                t.matmul(psB[0:16, 0:512], gpad[:, f, 16:32],
                         qbuf[:, f, 512:1024], start=(f == 0), stop=(f == KT - 1))
                _li = t.matmul(psB[0:16, 512:1024], gpad[:, f, 16:32],
                               qbuf[:, f, 1024:1536], start=(f == 0),
                               stop=(f == KT - 1))
            _li.then_inc(TT.inc(), 1)
        on("tensor", _tf)
        tt_f = TT.v

        def _sf(s, tt_f=tt_f, svw=SV.v):
            s.wait_ge(TT.h, tt_f)
            s.wait_ge(SV.h, svw)
            s.activation(out=glps[:, 0, :], in_=psA[0:16, :], func=AF.Identity)
            _li = s.activation(out=glps[:, 1, :], in_=psB[0:16, :],
                               func=AF.Identity)
            _li.then_inc(SS.inc(), 1)
        on("scalar", _sf)

        ss_gev = SS.v

        def _sg2(s, ss_gev=ss_gev):
            s.wait_ge(SS.h, ss_gev)
            s.activation(out=glps[:].rearrange("h a n -> h (a n)"),
                         in_=glps[:].rearrange("h a n -> h (a n)"),
                         func=AF.Exp, scale=-1.0).then_inc(SS.inc(), 1)
        on("scalar", _sg2)

        ss_gexp = SS.v

        def _syg(sy, ss_gexp=ss_gexp):
            sy.wait_ge(SS.h, ss_gexp)
            sy.dma_start_transpose(out=glcb[:, 0, 2:10, :], in_=glps[:, 0, :]
                                   ).then_inc(TP.inc(16), 16)
            sy.dma_start_transpose(out=glcb[:, 1, 2:10, :], in_=glps[:, 1, :]
                                   ).then_inc(TP.inc(16), 16)
        on("sync", _syg)
        tp_glt = TP.v

        def _vgc(v, tp_glt=tp_glt):
            v.wait_ge(TP.h, tp_glt)
            v.tensor_scalar(out=glcols[:, 2:10, :], in0=glcb[:, 0, 2:10, :],
                            scalar1=1.0, scalar2=None, op0=ALU.add)
            v.tensor_scalar(out=grcols[:, 2:10, :], in0=glcb[:, 1, 2:10, :],
                            scalar1=1.0, scalar2=None, op0=ALU.add)
            v.drain()
            v.reciprocal(out=glcols[:, 2:10, :], in_=glcols[:, 2:10, :])
            _li = v.reciprocal(out=grcols[:, 2:10, :], in_=grcols[:, 2:10, :])
            _li.then_inc(SV.inc(), 1)
        on("vector", _vgc)
        sv_glc = SV.v

        c_lo = max(0, 8 * sbi - 4)
        c_hi = (8 * sbi + 5) if not last else (NG - 1)
        sweep = list(range(c_lo, c_hi + 1))
        nsw = len(sweep)
        pv_lo = max(0, gorg)
        g_lo = pv_lo
        g_hi = c_hi
        ngr = g_hi - g_lo + 1

        def dblock(h):
            tev, sev, vev = [], [], []
            for ci in range(nsw):
                tev.append(("sc", h, ci))
                if ci >= 1 and sweep[ci - 1] >= pv_lo:
                    tev.append(("pv", h, ci - 1))
                sev.append(("exp", h, ci))
                vev.append(("mask", h, ci))
                if ci >= 1 and sweep[ci - 1] >= pv_lo:
                    vev.append(("evac", h, ci - 1))
            if sweep[nsw - 1] >= pv_lo:
                tev.append(("pv", h, nsw - 1))
                vev.append(("evac", h, nsw - 1))
            return tev, sev, vev

        def eblock(gi):
            tev, sev, vev = [], [], []
            for lh in range(LH):
                tev.append(("lsc", gi, lh))
                if lh >= 1:
                    tev.append(("rem", gi, lh - 1))
                sev.append(("lexp", gi, lh))
                vev.append(("lmask", gi, lh))
                if lh >= 1:
                    vev.append(("levac", gi, lh - 1))
            tev.append(("rem", gi, LH - 1))
            vev.append(("levac", gi, LH - 1))
            return tev, sev, vev

        T_ord, S_ord, V_ord = [], [], []
        for h in range(H):
            a, b, c = dblock(h)
            T_ord += a; S_ord += b; V_ord += c
            if h % 2 == 1 and h // 2 < ngr:
                a, b, c = eblock(h // 2)
                T_ord += a; S_ord += b; V_ord += c
        for gi in range(H // 2, ngr):
            a, b, c = eblock(gi)
            T_ord += a; S_ord += b; V_ord += c

        TT0, SS0, SV0 = TT.v, SS.v, SV.v
        tt_n, ss_n, sv_n = {}, {}, {}
        cnt = TT0
        for ev in T_ord:
            cnt += 1
            tt_n[ev] = cnt
        tt_end = cnt
        cnt = SS0
        for ev in S_ord:
            cnt += 1
            ss_n[ev] = cnt
        ss_end = cnt
        cnt = SV0
        for ev in V_ord:
            cnt += 1
            sv_n[ev] = cnt
        sv_end = cnt

        def _bigt(t, T_ord=T_ord, tt_n=tt_n, ss_n=ss_n, sv_n=sv_n, SS0=SS0,
                  sweep=sweep, nsw=nsw, pv_lo=pv_lo, org=org):
            for ev in T_ord:
                kind = ev[0]
                if kind == "sc":
                    _, h, ci = ev
                    C = sweep[ci]
                    dt_, half = h // 2, h % 2
                    kslot = C - org
                    N = min(384, S - 128 * C)
                    if ci >= 2:
                        pw = ss_n[("exp", h, ci - 2)]
                    elif h >= 1:
                        pw = ss_n[("exp", h - 1, ci - 2 + nsw)]
                    else:
                        pw = SS0
                    t.wait_ge(SS.h, pw)
                    _li = t.matmul(sc_ps[ci % 2][:, 0:N],
                                   kbuf[64 * half:64 * (half + 1), dt_,
                                        128 * kslot:128 * kslot + 128],
                                   qbuf[64 * half:64 * (half + 1), dt_,
                                        128 * kslot:128 * kslot + N],
                                   start=True, stop=True)
                elif kind == "pv":
                    _, h, ci = ev
                    g = sweep[ci]
                    t.wait_ge(SV.h, sv_n[("mask", h, ci)])
                    chunks = list(range(max(0, g - 2), g + 1))
                    for j, cc in enumerate(chunks):
                        _li = t.matmul(o65_ps[ci % 2],
                                       ebuf[:, cc % 4 + 4 * (h % 2),
                                            128 * (g - cc):128 * (g - cc) + 128],
                                       v65[:, cc - org, 65 * h:65 * (h + 1)],
                                       start=(j == 0),
                                       stop=(j == len(chunks) - 1))
                elif kind == "lsc":
                    _, gi, lh = ev
                    k = gi * LH + lh
                    ldt, lhalf = lh // 2, lh % 2
                    qoff = 128 * ((g_lo + gi) - gorg)
                    if lh >= 2:
                        pw = ss_n[("lexp", gi, lh - 2)]
                    elif gi >= 1:
                        pw = ss_n[("lexp", gi - 1, lh + LH - 2)]
                    else:
                        pw = SS0
                    t.wait_ge(SS.h, pw)
                    _li = t.matmul(lsc_ps[k % 2],
                                   gkfm[64 * lhalf:64 * lhalf + 64, ldt, 0:NL],
                                   gqbuf[64 * lhalf:64 * (lhalf + 1), ldt,
                                         qoff:qoff + 128],
                                   start=True, stop=True)
                else:
                    _, gi, lh = ev
                    k = gi * LH + lh
                    t.wait_ge(SV.h, sv_n[("lmask", gi, lh)])
                    _li = t.matmul(r65_ps[k % 2], lexps[0:NL, k % 2, :],
                                   gv65[0:NL, lh, :], start=True, stop=True)
                _li.then_inc(TT.inc(), 1)
        on("tensor", _bigt)
        assert TT.v == tt_end

        def _bigs(s, S_ord=S_ord, tt_n=tt_n, sweep=sweep, nsw=nsw, pv_lo=pv_lo):
            for ev in S_ord:
                if ev[0] == "exp":
                    _, h, ci = ev
                    C = sweep[ci]
                    N = min(384, S - 128 * C)
                    w = tt_n[("sc", h, ci)]
                    prev_pvs = [cj for cj in range(ci - 1)
                                if sweep[cj] >= pv_lo]
                    if prev_pvs:
                        w = max(w, tt_n[("pv", h, prev_pvs[-1])])
                    elif h >= 2:
                        w = max(w, tt_n[("pv", h - 2, nsw - 1)])
                    s.wait_ge(TT.h, w)
                    _li = s.activation(out=ebuf[:, C % 4 + 4 * (h % 2), 0:N],
                                       in_=sc_ps[ci % 2][:, 0:N],
                                       func=AF.Exp, scale=0.125)
                else:
                    _, gi, lh = ev
                    k = gi * LH + lh
                    w = tt_n[("lsc", gi, lh)]
                    if lh >= 2:
                        w = max(w, tt_n[("rem", gi, lh - 2)])
                    elif gi >= 1:
                        w = max(w, tt_n[("rem", gi - 1, lh + LH - 2)])
                    s.wait_ge(TT.h, w)
                    _li = s.activation(out=lexps[0:NL, k % 2, :],
                                       in_=lsc_ps[k % 2],
                                       func=AF.Exp, scale=0.125)
                _li.then_inc(SS.inc(), 1)
        on("scalar", _bigs)
        assert SS.v == ss_end

        stmx = prev_st_mixed[0]

        def _bigv(v, V_ord=V_ord, tt_n=tt_n, ss_n=ss_n, sweep=sweep,
                  stmx=stmx, gorg=gorg, g_lo=g_lo):
            v.wait_ge(ST.h, stmx)
            for ev in V_ord:
                kind = ev[0]
                if kind == "mask":
                    _, h, ci = ev
                    C = sweep[ci]
                    N = min(384, S - 128 * C)
                    es_ = C % 4 + 4 * (h % 2)
                    v.wait_ge(SS.h, ss_n[("exp", h, ci)])
                    _li = v.tensor_tensor(out=ebuf[:, es_, 0:N],
                                          in0=ebuf[:, es_, 0:N],
                                          in1=mask_loc[:, 0:N], op=ALU.mult)
                elif kind == "evac":
                    _, h, ci = ev
                    g = sweep[ci]
                    gsl = g - gorg
                    par2 = ci % 2
                    v.wait_ge(TT.h, tt_n[("pv", h, ci)])
                    v.reciprocal(out=fcol[:, par2:par2 + 1],
                                 in_=o65_ps[par2][:, 64:65])
                    v.drain()
                    _li = v.scalar_tensor_tensor(
                        out=mixed_tm[:, gsl, 64 * h:64 * (h + 1)],
                        in0=o65_ps[par2][:, 0:64],
                        scalar=fcol[:, par2:par2 + 1],
                        in1=glcols[:, gsl, h:h + 1].broadcast_to([128, 64]),
                        op0=ALU.mult, op1=ALU.mult)
                elif kind == "lmask":
                    _, gi, lh = ev
                    k = gi * LH + lh
                    g = g_lo + gi
                    v.wait_ge(SS.h, ss_n[("lexp", gi, lh)])
                    _li = v.tensor_mask(out=lexps[0:NL, k % 2, :],
                                        in_=lexps[0:NL, k % 2, :],
                                        mask_length=latthr[0:NL, g:g + 1],
                                        mask_offsets=latnegq[0:NL, :], offset=0)
                else:
                    _, gi, lh = ev
                    k = gi * LH + lh
                    gsl = (g_lo + gi) - gorg
                    v.wait_ge(TT.h, tt_n[("rem", gi, lh)])
                    v.reciprocal(out=fcol[:, 2 + k % 2:3 + k % 2],
                                 in_=r65_ps[k % 2][:, 64:65])
                    v.drain()
                    _li = v.tensor_tensor(
                        out=remp_tm[:, gsl, 64 * lh:64 * (lh + 1)],
                        in0=r65_ps[k % 2][:, 0:64],
                        in1=fcol[:, 2 + k % 2:3 + k % 2].broadcast_to([128, 64]),
                        op=ALU.mult)
                _li.then_inc(SV.inc(), 1)
        on("vector", _bigv)
        assert SV.v == sv_end

        ss_f = {gi: ss_end + gi + 1 for gi in range(ngr)}
        sv_fv = {gi: sv_end + gi + 1 for gi in range(ngr)}
        tt_r = {}
        for gi, g in enumerate(range(g_lo, g_hi + 1)):
            gsl = g - gorg
            tt_prev = tt_r.get(gi - 1, 0)

            def _syf(sy, gsl=gsl, w=sv_n[("levac", gi, LH - 1)], tt_prev=tt_prev):
                sy.wait_ge(SV.h, w)
                if tt_prev:
                    sy.wait_ge(TT.h, tt_prev)
                sy.dma_start_transpose(out=remp_fm[:, :, :],
                                       in_=remp_tm[:, gsl, :]).then_inc(TP.inc(16), 16)
            on("sync", _syf)
            tp_r = TP.v

            def _tf2(t, tp_r=tp_r, gi=gi, sv_end=sv_end):
                t.wait_ge(TP.h, tp_r)
                if gi == 0:
                    t.wait_ge(SV.h, sv_end)
                else:
                    t.wait_ge(SS.h, ss_f[gi - 1])
                for lt in range(LT):
                    t.matmul(psA[:, 0:512], remp_fm[:, lt, :], routw[:, lt, 0:512],
                             start=(lt == 0), stop=(lt == LT - 1))
                    _li = t.matmul(psA[:, 512:1024], remp_fm[:, lt, :],
                                   routw[:, lt, 512:1024],
                                   start=(lt == 0), stop=(lt == LT - 1))
                _li.then_inc(TT.inc(), 1)
            on("tensor", _tf2)
            tt_r[gi] = TT.v

        def _fs(s, items=[(gi, g) for gi, g in enumerate(range(g_lo, g_hi + 1))],
                tt_r=dict(tt_r), sv_fv=sv_fv, gorg=gorg):
            for gi, g in items:
                gsl = g - gorg
                s.wait_ge(TT.h, tt_r[gi])
                if gi >= 1:
                    s.wait_ge(SV.h, sv_fv[gi - 1])
                for h in range(H):
                    _li = s.activation(out=remg[:, 64 * h:64 * (h + 1)],
                                       in_=psA[:, 64 * h:64 * (h + 1)],
                                       func=AF.Identity,
                                       scale=grcols[:, gsl, h:h + 1])
                _li.then_inc(SS.inc(), 1)
        on("scalar", _fs)
        assert SS.v == ss_end + ngr

        def _fv(v, items=[(gi, g) for gi, g in enumerate(range(g_lo, g_hi + 1))],
                ss_f=ss_f, gorg=gorg):
            for gi, g in items:
                gsl = g - gorg
                v.wait_ge(SS.h, ss_f[gi])
                _li = v.tensor_tensor(out=mixed_tm[:, gsl, :],
                                      in0=mixed_tm[:, gsl, :],
                                      in1=remg[:], op=ALU.add)
                _li.then_inc(SV.inc(), 1)
        on("vector", _fv)
        assert SV.v == sv_end + ngr

        sv_f = SV.v
        sl_lo = g_lo - gorg
        n_sl = g_hi - g_lo + 1

        def _sy(sy, sv_f=sv_f, sl_lo=sl_lo, n_sl=n_sl, g_lo=g_lo):
            sy.wait_ge(SV.h, sv_f)
            sy.dma_start(out=mixed_h[128 * g_lo:128 * (g_lo + n_sl), :]
                         .rearrange("(a p) d -> p a d", p=128),
                         in_=mixed_tm[:, sl_lo:sl_lo + n_sl, :]
                         ).then_inc(ST.inc(16), 16)
        on("sync", _sy)
        prev_st_mixed[0] = ST.v

        if not last:
            def _vt(v):
                v.wait_ge(SV.h, SV.v)
                v.wait_ge(SS.h, SS.v)
                v.tensor_copy(out=kbuf[:, :, 0:512], in_=kbuf[:, :, 1024:1536])
                v.tensor_copy(out=qbuf[:, :, 0:512], in_=qbuf[:, :, 1024:1536])
                v.tensor_copy(out=gqbuf[:, :, 0:256], in_=gqbuf[:, :, 1024:1280])
                v.tensor_copy(out=v65[:, 0:4, :], in_=v65[:, 8:12, :])
                v.tensor_copy(out=glcols[:, 0:2, :], in_=glcols[:, 8:10, :])
                _li = v.tensor_copy(out=grcols[:, 0:2, :], in_=grcols[:, 8:10, :])
                _li.then_inc(SV.inc(), 1)
            on("vector", _vt)
            sv_tail = SV.v

            def _tb(t, sv_tail=sv_tail):
                t.wait_ge(SV.h, sv_tail)
            on("tensor", _tb)

    if debug == 2:
        dbg_names = [("qbuf", qbuf), ("kbuf", kbuf), ("v65", v65),
                     ("glcols", glcols), ("grcols", grcols), ("gv65", gv65),
                     ("gkfm", gkfm), ("gqbuf", gqbuf), ("ebuf", ebuf),
                     ("remp", remp_tm), ("mixtm", mixed_tm), ("fcold", fcol)]
        dbg_h = {}
        for nm, tsr in dbg_names:
            ap = tsr if not hasattr(tsr, "ap") else tsr.ap()
            shape = list(ap.shape) if hasattr(ap, "shape") else list(tsr.shape)
            dt_ = ap.dtype
            dbg_h[nm] = nc.dram_tensor("dbg_" + nm, shape, dt_,
                                       kind="ExternalOutput").ap()

        def _dmp(sy):
            sy.wait_ge(SV.h, SV.v)
            sy.wait_ge(SS.h, SS.v)
            sy.wait_ge(TT.h, TT.v)
            for nm, tsr in dbg_names:
                ap = tsr if not hasattr(tsr, "ap") else tsr.ap()
                sy.dma_start(out=dbg_h[nm][:], in_=ap[:] if hasattr(ap, "shape") else ap
                             ).then_inc(ST.inc(16), 16)
            sy.wait_ge(ST.h, ST.v)
        on("sync", _dmp)
        es1.close()
        es.close()
        return nc

    es1.close()

    p1_sv, p1_ss, p1_tt, p1_gg = SV.v, SS.v, TT.v, GG.v
    p1_tp, p1_st, p1_qd, p1_gd = TP.v, ST.v, QD.v, GD.v

    es3 = contextlib.ExitStack()
    def sb3(name, shape, dt):
        return es3.enter_context(nc.sbuf_tensor(name, shape, dt))
    wgT = sb3("wgT", [128, KT, I], BF16)
    wuT = sb3("wuT", [128, KT, I], BF16)
    wdT = sb3("wdT", [128, MT, D], BF16)
    woutw = sb3("woutw", [128, KT, D], BF16)
    xqr = sb3("xqr", [128, 2, D], BF16)
    xqf = sb3("xqf", [128, 2, KT, 128], BF16)
    mxr = sb3("mxr", [128, 2, D], BF16)
    mxf = sb3("mxf", [128, 2, KT, 128], BF16)
    hbuf = sb3("hbuf", [128, D], F32)
    hq = sb3("hq", [128, MT, 128], BF16)
    hqt = sb3("hqt", [128, I], BF16)
    xres = sb3("xres", [128, 2, D], F32)

    def _barrier(e):
        e.wait_ge(SV.h, p1_sv)
        e.wait_ge(SS.h, p1_ss)
        e.wait_ge(TT.h, p1_tt)
        e.wait_ge(GG.h, p1_gg)
        e.wait_ge(TP.h, p1_tp)
        e.wait_ge(ST.h, p1_st)
        e.wait_ge(QD.h, p1_qd)
        e.wait_ge(GD.h, p1_gd)

    def _syl(sy):
        _barrier(sy)
        for k in range(KT):
            sy.dma_start(out=wgT[:, k, :], in_=wgT_h[k]).then_inc(QD.inc(16), 16)
            sy.dma_start(out=wuT[:, k, :], in_=wuT_h[k]).then_inc(QD.inc(16), 16)
        for m in range(MT):
            sy.dma_start(out=wdT[:, m, :], in_=wdT_h[m]).then_inc(QD.inc(16), 16)
        sy.wait_ge(QD.h, QD.v)
    on("sync", _syl)

    def _gl(g):
        _barrier(g)
        for k in range(KT):
            g.dma_start(out=woutw[:, k, :], in_=wout_h[k * 128:(k + 1) * 128, :]
                        ).then_inc(GD.inc(16), 16)
        g.wait_ge(GD.h, GD.v)
    on("gpsimd", _gl)
    p3_qd = QD.v
    p3_gd = GD.v

    for eng in ["tensor", "vector", "scalar"]:
        def _b(e):
            _barrier(e)
        on(eng, _b)

    ngs = [(0, 1024), (1024, 2048), (2048, 2816)]
    NT = S // 128
    SS3 = SS.v
    SVB, GGB = SV.v, GG.v
    R = {}

    def gg_cf(st, ci):
        return GGB + 3 * st + ci + 1

    def emit_A(st):
        slot = st % 2
        tok = st * 128

        def _sy(sy, slot=slot, tok=tok, st=st):
            if st >= 2:
                sy.wait_ge(TP.h, R[st - 2, "tp_hq"])
            elif st == 1:
                sy.wait_ge(TP.h, R[0, "tp_pair"])
            if st >= 2:
                sy.wait_ge(ST.h, R[st - 2, "st_store"])
                sy.wait_ge(TT.h, R[st - 2, "tt_out"])
            sy.dma_start(out=xqr[:, slot, :], in_=xq_h[tok:tok + 128, :]
                         ).then_inc(QD.inc(16), 16)
            sy.dma_start(out=mxr[:, slot, :], in_=mixed_h[tok:tok + 128, :]
                         ).then_inc(QD.inc(16), 16)
            sy.dma_start(out=xres[:, slot, :], in_=x_h[tok:tok + 128, :]
                         ).then_inc(QD.inc(16), 16)
            sy.wait_ge(QD.h, QD.v)
            sy.dma_start_transpose(out=xqf[:, slot, :, :], in_=xqr[:, slot, :]
                                   ).then_inc(TP.inc(16), 16)
            sy.dma_start_transpose(out=mxf[:, slot, :, :], in_=mxr[:, slot, :]
                                   ).then_inc(TP.inc(16), 16)
        on("sync", _sy)
        R[st, "qd"] = QD.v
        R[st, "tp_pair"] = TP.v

    def emit_H(st):
        def _sy2(sy, st=st):
            sy.wait_ge(SV.h, R[st, "sv_v3_2"])
            sy.wait_ge(TP.h, R.get((st + 1, "tp_pair"),
                                   R[st, "tp_pair"]))
            if st >= 1:
                sy.wait_ge(TT.h, R[st - 1, "tt_out"])
            sy.dma_start_transpose(out=hq[:, :, :], in_=hqt[:]
                                   ).then_inc(TP.inc(16), 16)
        on("sync", _sy2)
        R[st, "tp_hq"] = TP.v

    def emit_tile_chunks(st):
        slot = st % 2
        for ci, (n0, n1) in enumerate(ngs):
            W = n1 - n0
            ssw = SS3 + 3 * st + ci
            if ci == 0:
                svw = R[st - 1, "sv_hm_2"] if st >= 1 else 0
            else:
                svw = R[st, "sv_hm_%d" % (ci - 1)]

            def _t(t, slot=slot, n0=n0, W=W, st=st, ssw=ssw, svw=svw):
                t.wait_ge(TP.h, R[st, "tp_pair"])
                t.wait_ge(QD.h, p3_qd)
                t.wait_ge(SS.h, ssw)
                t.wait_ge(SV.h, svw)
                for k in range(KT):
                    for o in range(0, W, 512):
                        oe = min(W, o + 512)
                        t.matmul(psA[:, o:oe], xqf[:, slot, k, :],
                                 wgT[:, k, n0 + o:n0 + oe],
                                 start=(k == 0), stop=(k == KT - 1))
                        _li = t.matmul(psB[:, o:oe], xqf[:, slot, k, :],
                                       wuT[:, k, n0 + o:n0 + oe],
                                       start=(k == 0), stop=(k == KT - 1))
                _li.then_inc(TT.inc(), 1)
            on("tensor", _t)
            R[st, "tt_gu_%d" % ci] = TT.v

            def _v(v, W=W, st=st, ci=ci):
                v.wait_ge(SS.h, SS3 + 3 * st + ci + 1)
                _li = v.tensor_tensor(out=hbuf[:, 0:W], in0=hbuf[:, 0:W],
                                      in1=psB[:, 0:W], op=ALU.mult)
                _li.then_inc(SV.inc(), 1)
            on("vector", _v)
            R[st, "sv_hm_%d" % ci] = SV.v
            dwl = [(TP.h, R[st - 1, "tp_hq"])] if st >= 1 else None
            emit_quant(slot, W, hqt[:, n0:n1], src_ap=hbuf[:, 0:W],
                       lead_drain=True, dst_waits=dwl)
            R[st, "sv_v1_%d" % ci] = SV.v - 1
            R[st, "sv_v3_%d" % ci] = SV.v

    def emit_out(st):
        slot = st % 2

        def _t2(t, slot=slot, st=st):
            t.wait_ge(TP.h, R[st, "tp_hq"])
            t.wait_ge(GD.h, p3_gd)
            if st >= 1:
                t.wait_ge(SV.h, R[st - 1, "sv_out"])
            for k in range(KT):
                t.matmul(psC[:, 0:512], mxf[:, slot, k, :], woutw[:, k, 0:512],
                         start=(k == 0), stop=False)
                t.matmul(psC[:, 512:1024], mxf[:, slot, k, :],
                         woutw[:, k, 512:1024], start=(k == 0), stop=False)
            for m in range(MT):
                t.matmul(psC[:, 0:512], hq[:, m, :], wdT[:, m, 0:512],
                         start=False, stop=(m == MT - 1))
                _li = t.matmul(psC[:, 512:1024], hq[:, m, :],
                               wdT[:, m, 512:1024],
                               start=False, stop=(m == MT - 1))
            _li.then_inc(TT.inc(), 1)
        on("tensor", _t2)
        R[st, "tt_out"] = TT.v

    def emit_outadd(st):
        slot = st % 2

        def _v2(v, slot=slot, st=st):
            v.wait_ge(TT.h, R[st, "tt_out"])
            v.wait_ge(QD.h, R[st, "qd"])
            _li = v.tensor_tensor(out=xres[:, slot, :], in0=psC[:],
                                  in1=xres[:, slot, :], op=ALU.add)
            _li.then_inc(SV.inc(), 1)
        on("vector", _v2)
        R[st, "sv_out"] = SV.v

    def emit_store(st):
        slot = st % 2
        tok = st * 128

        def _sy3(sy, slot=slot, tok=tok, st=st):
            sy.wait_ge(SV.h, R[st, "sv_out"])
            if st >= 1:
                sy.wait_ge(ST.h, R[st - 1, "st_store"])
            sy.dma_start(out=out_h[tok:tok + 128, :], in_=xres[:, slot, :]
                         ).then_inc(ST.inc(16), 16)
        on("sync", _sy3)
        R[st, "st_store"] = ST.v

    for st in range(NT):
        emit_A(st)
        if st >= 1:
            emit_H(st - 1)
        emit_tile_chunks(st)
        if st >= 1:
            emit_out(st - 1)
            emit_outadd(st - 1)
            emit_store(st - 1)
    emit_H(NT - 1)
    emit_out(NT - 1)
    emit_outadd(NT - 1)
    emit_store(NT - 1)
    assert GG.v == gg_cf(NT - 1, 2)

    def _sall(s):
        for st in range(NT):
            for ci, (n0, n1) in enumerate(ngs):
                W = n1 - n0
                s.wait_ge(TT.h, R[st, "tt_gu_%d" % ci])
                if ci > 0:
                    s.wait_ge(SV.h, R[st, "sv_v1_%d" % (ci - 1)])
                    s.wait_ge(GG.h, gg_cf(st, ci - 1))
                elif st > 0:
                    s.wait_ge(SV.h, R[st - 1, "sv_v1_2"])
                    s.wait_ge(GG.h, gg_cf(st - 1, 2))
                s.activation(out=hbuf[:, 0:W], in_=psA[:, 0:W], func=AF.Silu
                             ).then_inc(SS.inc(), 1)
    on("scalar", _sall)
    assert SS.v == SS3 + 3 * NT

    st_fin = ST.v

    def _fin(sy, st_fin=st_fin):
        sy.wait_ge(ST.h, st_fin)
    on("sync", _fin)

    es3.close()
    es.close()
    return nc


_NC_CACHE = {}


def _get_nc(S):
    if S not in _NC_CACHE:
        nc = build_core_program(S)
        mybir.codegen_inst_isa_subclasses(nc)
        _NC_CACHE[S] = nc
    return _NC_CACHE[S]


def kernel(L0_attn_x, L0_attn_wq, L0_attn_wk, L0_attn_wv, L0_attn_wgq,
           L0_attn_wgk, L0_attn_wgv, L0_attn_r_out_w, L0_attn_gate_local_w,
           L0_attn_gate_remote_w, L0_attn_w_out, L0_ffn_w_gate, L0_ffn_w_up,
           L0_ffn_w_down):
    x = np.ascontiguousarray(np.asarray(L0_attn_x))
    B, S, D = x.shape
    nc = _get_nc(S)
    common = {
        "wq": np.ascontiguousarray(L0_attn_wq),
        "wk": np.ascontiguousarray(L0_attn_wk),
        "wv": np.ascontiguousarray(L0_attn_wv),
        "wgq": np.ascontiguousarray(L0_attn_wgq),
        "wgk": np.ascontiguousarray(L0_attn_wgk),
        "wgv": np.ascontiguousarray(L0_attn_wgv),
        "r_out_w": np.ascontiguousarray(L0_attn_r_out_w),
        "glw": np.ascontiguousarray(L0_attn_gate_local_w),
        "grw": np.ascontiguousarray(L0_attn_gate_remote_w),
        "w_out": np.ascontiguousarray(L0_attn_w_out),
        "w_gate": np.ascontiguousarray(L0_ffn_w_gate),
        "w_up": np.ascontiguousarray(L0_ffn_w_up),
        "w_down": np.ascontiguousarray(L0_ffn_w_down),
    }
    in_maps = [dict(common, x=x[i]) for i in range(B)]
    res = run_bass_kernel_spmd(nc, in_maps, core_ids=list(range(B)))
    out = np.stack([np.asarray(res.results[i]["out"]) for i in range(B)], axis=0)
    return out.astype(np.float32)


# version 10 (final)
# speedup vs baseline: 1.0199x; 1.0199x over previous
import contextlib
import numpy as np
import ml_dtypes
import concourse.bass as bass
import concourse.mybir as mybir
from concourse.bass_utils import run_bass_kernel_spmd

F32 = mybir.dt.float32
BF16 = mybir.dt.bfloat16
AF = mybir.ActivationFunctionType
ALU = mybir.AluOpType
AX = mybir.AxisListType

_T1, _T2, _T3, _T4 = 0.0625, 0.5625, 1.5625, 3.0625
_T5, _T6, _T7 = 6.25, 12.25, 25.0

_OPS = {}


def _register_fp4_ops():
    if _OPS:
        return _OPS["S1"], _OPS["S2"]
    from concourse.dve_op import DveOp
    import concourse.dve_ops as dve_ops
    from concourse.dve_spec import (
        Spec, Src0, Src1, C0, C1, C2, C3, lower, _spill_c3_to_src1)
    from concourse.dve_table_gen import dve_ver_for

    def _mk(name, spec, ref, row):
        ver = dve_ver_for("TRN2")
        sha = lower(spec, ver=ver, perf_en=False, name=name).sha(ver)
        op = DveOp(name, spec, subdim=False, uops_sha={ver: sha}, reference=ref)
        if all(o.name != name for o in dve_ops.OPS):
            dve_ops.OPS.append(op)
            dve_ops._SUB_OPCODE_FOR_NAME[name] = row
            dve_ops.CUSTOM_DVE_SPECS[name] = spec
            dve_ops.CUSTOM_DVE_REFS[name] = ref
        return op

    w = Src0
    body1 = _spill_c3_to_src1(((w > C0) + (w > C1)) + ((w > C2) + (w > C3)))

    def ref1(in0, in1, s0, s1, imm2):
        v = in0
        r = ((v > s0).astype(in0.dtype) + (v > s1) + (v > imm2) + (v > in1))
        return r.astype(in0.dtype)

    c3_ = w > C2
    g = ((w > C0) + (w > C1)) + (c3_ + c3_)
    body2 = (g + g) + Src1

    def ref2(in0, in1, s0, s1, imm2):
        v = in0
        gg = (v > s0).astype(in0.dtype) + (v > s1) + 2.0 * (v > imm2)
        return (2.0 * gg + in1).astype(in0.dtype)

    _OPS["S1"] = _mk("ANT_FP4_S1", Spec(body=body1), ref1, 30)
    _OPS["S2"] = _mk("ANT_FP4_S2", Spec(body=body2), ref2, 31)
    return _OPS["S1"], _OPS["S2"]


class Sem:
    def __init__(self, h):
        self.h = h
        self.v = 0

    def inc(self, n=1):
        self.v += n
        return self.h


def build_core_program(S=4096, debug=False, nonce=0):
    op1, op2 = _register_fp4_ops()
    D, I, H, LH, LD = 1024, 2816, 16, 8, 512
    NL = S // 64
    NSB = S // 1024
    NG = S // 128
    KT = D // 128
    MT = I // 128
    LT = LD // 128

    nc = bass.Bass("TRN2", target_bir_lowering=False)

    x_h = nc.dram_tensor("x", [S, D], F32, kind="ExternalInput").ap()
    if nonce:
        nc.dram_tensor("nonce_in", [1, 64 + nonce], F32, kind="ExternalInput").ap()
    wq_h = nc.dram_tensor("wq", [D, D], F32, kind="ExternalInput").ap()
    wk_h = nc.dram_tensor("wk", [D, D], F32, kind="ExternalInput").ap()
    wv_h = nc.dram_tensor("wv", [D, D], F32, kind="ExternalInput").ap()
    wgq_h = nc.dram_tensor("wgq", [D, LD], F32, kind="ExternalInput").ap()
    wgk_h = nc.dram_tensor("wgk", [D, LD], F32, kind="ExternalInput").ap()
    wgv_h = nc.dram_tensor("wgv", [D, LD], F32, kind="ExternalInput").ap()
    rout_h = nc.dram_tensor("r_out_w", [LD, D], F32, kind="ExternalInput").ap()
    glw_h = nc.dram_tensor("glw", [H, 64], F32, kind="ExternalInput").ap()
    grw_h = nc.dram_tensor("grw", [H, 64], F32, kind="ExternalInput").ap()
    wout_h = nc.dram_tensor("w_out", [D, D], F32, kind="ExternalInput").ap()
    wg_h = nc.dram_tensor("w_gate", [I, D], F32, kind="ExternalInput").ap()
    wu_h = nc.dram_tensor("w_up", [I, D], F32, kind="ExternalInput").ap()
    wd_h = nc.dram_tensor("w_down", [D, I], F32, kind="ExternalInput").ap()
    out_h = nc.dram_tensor("out", [S, D], F32, kind="ExternalOutput").ap()
    _scr = "ExternalOutput" if debug else "Internal"
    xq_h = nc.dram_tensor("xq_scr", [S, D], BF16, kind=_scr).ap()
    mixed_h = nc.dram_tensor("mixed_scr", [S, D], BF16, kind=_scr).ap()
    wgT_h = nc.dram_tensor("wgT_scr", [KT, 128, I], BF16, kind=_scr).ap()
    wuT_h = nc.dram_tensor("wuT_scr", [KT, 128, I], BF16, kind="Internal").ap()
    wdT_h = nc.dram_tensor("wdT_scr", [MT, 128, D], BF16, kind="Internal").ap()

    es = contextlib.ExitStack()

    def sb(name, shape, dt):
        return es.enter_context(nc.sbuf_tensor(name, shape, dt))

    qx2 = sb("qx2", [128, 1, D], F32)
    qsgn = sb("qsgn", [128, 2, D], BF16)
    qw = sb("qw", [128, 1, D], F32)
    qA = sb("qA", [128, 1, D], BF16)
    qH = sb("qH", [128, 1, D], BF16)
    qM = sb("qM", [128, 1, D], BF16)
    qam = sb("qam", [128, 2, 64], F32)
    qs6 = sb("qs6", [128, 2, 64], F32)
    qri = sb("qri", [128, 2, 64], F32)
    qri2 = sb("qri2", [128, 2, 64], F32)
    qhm = sb("qhm", [128, 2, 64], BF16)
    c3col = sb("c3col", [128, 1], F32)

    psA = es.enter_context(nc.psum_tensor("psA", [128, 1024], F32))
    psB = es.enter_context(nc.psum_tensor("psB", [128, 1024], F32))
    psC = es.enter_context(nc.psum_tensor("psC", [128, 1024], F32))
    psD = es.enter_context(nc.psum_tensor("psD", [128, 1024], F32))

    sems = {n: Sem(es.enter_context(nc.semaphore(name="sem_" + n)))
            for n in ["qd", "gd", "tp", "st", "v", "s", "t", "g", "w0", "w1"]}
    QD, GD, TP, ST = sems["qd"], sems["gd"], sems["tp"], sems["st"]
    SV, SS, TT, GG = sems["v"], sems["s"], sems["t"], sems["g"]
    WP = [sems["w0"], sems["w1"]]

    es1 = contextlib.ExitStack()
    def sb1(name, shape, dt):
        return es1.enter_context(nc.sbuf_tensor(name, shape, dt))
    stat = sb1("stat", [128, 2, 12], F32)
    aggr = sb1("aggr", [128, 2, 2], F32)
    lncol = sb1("lncol", [128, 8], F32)
    gpad = sb1("gpad", [128, 8, 32], BF16)
    glw_sb = sb1("glw_sb", [16, 2, 64], F32)
    glw_T = sb1("glw_T", [64, 2 * H], BF16)
    mask_loc = sb1("mask_loc", [128, 384], BF16)
    pcol = sb1("pcol", [128, 1], F32)
    latthr = sb1("latthr", [64, NG], F32)
    latnegq = sb1("latnegq", [64, 128], F32)
    fcol = sb1("fcol", [128, 4], F32)
    idmat = sb1("idmat", [128, 128], F32)
    routw = sb1("routw", [128, LT, D], BF16)
    xin = sb1("xin", [128, 2, D], F32)
    xn_sb = sb1("xn_sb", [128, 2, D], BF16)
    xq_sb = sb1("xq_sb", [128, 2, D], BF16)
    xn_fm = sb1("xn_fm", [128, KT, 1024], BF16)
    wpan = sb1("wpan", [128, 2, KT, 128], BF16)
    qbuf = sb1("qbuf", [128, KT, 1536], BF16)
    kbuf = sb1("kbuf", [128, KT, 1536], BF16)
    vtmp = sb1("vtmp", [128, D], BF16)
    vtsc = sb1("vtsc", [128, 8, 128], BF16)
    v65 = sb1("v65", [128, 12, H * 65], BF16)
    ebuf = sb1("ebuf", [128, 8, 384], BF16)
    gqbuf = sb1("gqbuf", [128, LT, 1280], BF16)
    glcb = sb1("glcb", [128, 2, 10, H], BF16)
    glps = qsgn[0:16, :, :]
    glcols = sb1("glcols", [128, 10, H], F32)
    grcols = sb1("grcols", [128, 10, H], F32)
    pooled = sb1("pooled", [128, KT, 16], BF16)
    pooledf = sb1("pooledf", [128, KT, 16], F32)
    gkfm = sb1("gkfm", [128, LT, NL], BF16)
    gv65 = sb1("gv65", [64, LH, 65], BF16)
    gvfm = sb1("gvfm", [128, LT, NL], F32)
    mixed_tm = sb1("mixed_tm", [128, 10, D], BF16)
    remp_tm = sb1("remp_tm", [128, 10, LD], BF16)
    remp_fm = sb1("remp_fm", [128, LT, 128], BF16)
    remg = sb1("remg", [128, D], BF16)
    fw32 = sb1("fw32", [128, 1, D], F32)
    fwq = sb1("fwq", [128, 1, D], BF16)
    fwT = sb1("fwT", [128, KT, 128], BF16)
    lexps = sb1("lexps", [64, 2, 128], BF16)

    fidx = qx2[:, 0, 0:384]
    mtmp = qx2[:, 0, 384:768]

    block = es.enter_context(nc.Block(no_gpsimd_drain=True))

    def on(name, fn):
        getattr(block, name)(fn)

    def _g(g):
        g.iota(fidx, pattern=[[1, 384]], base=0, channel_multiplier=0,
               allow_small_or_imprecise_dtypes=True)
        g.iota(pcol[:], pattern=[[0, 1]], base=0, channel_multiplier=1,
               allow_small_or_imprecise_dtypes=True)
        g.iota(latthr[:], pattern=[[128, NG]], base=1, channel_multiplier=-64,
               allow_small_or_imprecise_dtypes=True)
        g.iota(latnegq[:], pattern=[[-1, 128]], base=0, channel_multiplier=0,
               allow_small_or_imprecise_dtypes=True).then_inc(GG.inc(), 1)
        for k in range(LT):
            g.dma_start(out=routw[:, k, :], in_=rout_h[k * 128:(k + 1) * 128, :]
                        ).then_inc(GD.inc(16), 16)
    on("gpsimd", _g)

    def _gw(g):
        g.dma_start(out=glw_sb[:, 0, :], in_=glw_h[:]).then_inc(GD.inc(16), 16)
        g.wait_ge(GD.h, GD.v)
        g.dma_start(out=glw_sb[:, 1, :], in_=grw_h[:]).then_inc(GD.inc(16), 16)
    on("gpsimd", _gw)
    gd_glw = GD.v

    def _vid(v, gd_glw=gd_glw):
        v.wait_ge(GG.h, GG.v)
        v.scalar_tensor_tensor(out=idmat[:], in0=fidx[:, 0:128][0:128, 0:128],
                               scalar=0.0, in1=pcol[:].broadcast_to([128, 128]),
                               op0=ALU.add, op1=ALU.is_equal)
        v.drain().then_inc(SV.inc(), 1)
    on("vector", _vid)
    sv_id = SV.v

    def _tgw(t, gd_glw=gd_glw, sv_id=sv_id):
        t.wait_ge(GD.h, gd_glw)
        t.wait_ge(SV.h, sv_id)
        t.transpose(psA[0:64, 0:16], glw_sb[:, 0, :], idmat[0:16, 0:16])
        _li = t.transpose(psB[0:64, 0:16], glw_sb[:, 1, :], idmat[0:16, 0:16])
        _li.then_inc(TT.inc(), 1)
    on("tensor", _tgw)
    tt_glw = TT.v

    def _sgw(s, tt_glw=tt_glw):
        s.wait_ge(TT.h, tt_glw)
        s.activation(out=glw_T[:, 0:16], in_=psA[0:64, 0:16], func=AF.Identity)
        _li = s.activation(out=glw_T[:, 16:32], in_=psB[0:64, 0:16],
                           func=AF.Identity)
        _li.then_inc(SS.inc(), 1)
    on("scalar", _sgw)
    ss_glw = SS.v

    def _v(v):
        v.memset(c3col[:], _T4)
        v.memset(v65[:], 0.0)
        v.memset(gkfm[:], 0.0)
        v.memset(gvfm[:], 0.0)
        v.memset(gv65[:], 0.0)
        v.drain()
        v.memset(v65[:].rearrange("p a (h c) -> p a h c", c=65)[:, :, :, 64:65], 1.0)
        v.memset(gv65[:, :, 64:65], 1.0)
        v.wait_ge(GG.h, GG.v)
        v.scalar_tensor_tensor(out=mtmp, in0=fidx, scalar=0.0,
                               in1=pcol[:].broadcast_to([128, 384]),
                               op0=ALU.add, op1=ALU.is_ge)
        v.drain()
        v.scalar_tensor_tensor(out=fidx, in0=fidx, scalar=-256.0,
                               in1=pcol[:].broadcast_to([128, 384]),
                               op0=ALU.add, op1=ALU.is_lt)
        v.drain()
        v.tensor_tensor(out=mask_loc[:], in0=mtmp, in1=fidx, op=ALU.mult)
        v.wait_ge(SS.h, ss_glw)
        v.memset(gpad[:], 0.0)
        v.drain()
        gpf = gpad[:].rearrange("p a b -> p (a b)")
        for j in range(2):
            v.tensor_copy(out=gpf[64 * j:64 * (j + 1), j:j + 34 * 7 + 1:34],
                          in_=glw_T[:, j:16:2])
            v.tensor_copy(out=gpf[64 * j:64 * (j + 1), 16 + j:16 + j + 34 * 7 + 1:34],
                          in_=glw_T[:, 16 + j:32:2])
        v.drain().then_inc(SV.inc(), 1)
    on("vector", _v)
    base_sv = SV.v


    def emit_quant(slot, W, dst_ap, src_ap, dst_waits=None, src_waits=None,
                   lead_drain=False):
        nb = W // 16
        x2 = qx2[:, 0, 0:W]
        sg = qsgn[:, slot, 0:W]
        wv = qw[:, 0, 0:W]
        Av = qA[:, 0, 0:W]
        Hv = qH[:, 0, 0:W]
        Mv = qM[:, 0, 0:W]
        am = qam[:, slot, 0:nb]
        s6 = qs6[:, slot, 0:nb]
        ri = qri[:, slot, 0:nb]
        ri2 = qri2[:, slot, 0:nb]
        hm = qhm[:, slot, 0:nb]

        def _v1(v):
            if src_waits:
                for sh, sv_ in src_waits:
                    v.wait_ge(sh, sv_)
            if lead_drain:
                v.drain()
            v.tensor_reduce(out=am, in_=src_ap.rearrange("p (nb k) -> p nb k", k=16),
                            axis=AX.X, op=ALU.max, apply_absolute_value=True)
            v.tensor_scalar(out=sg, in0=src_ap, scalar1=0.0, scalar2=2.0,
                            op0=ALU.is_gt, op1=ALU.mult)
            v.drain()
            v.tensor_scalar(out=s6, in0=am, scalar1=1.0 / 6.0, scalar2=1e-12,
                            op0=ALU.mult, op1=ALU.max)
            v.tensor_scalar(out=hm, in0=am, scalar1=1.0 / 12.0, scalar2=5e-13,
                            op0=ALU.mult, op1=ALU.max)
            v.drain()
            _li = v.reciprocal(out=ri, in_=s6)
            _li.then_inc(SV.inc(), 1)
        on("vector", _v1)

        def _gq(g):
            g.wait_ge(SV.h, SV.v)
            g.tensor_tensor(out=ri2, in0=ri, in1=ri, op=ALU.mult)
            g.tensor_tensor(out=x2, in0=src_ap, in1=src_ap, op=ALU.mult)
            g.drain()
            r2b = ri2.unsqueeze(2).broadcast_to([128, nb, 16])
            g.tensor_tensor(out=wv.rearrange("p (nb k) -> p nb k", k=16),
                            in0=x2.rearrange("p (nb k) -> p nb k", k=16),
                            in1=r2b, op=ALU.mult).then_inc(GG.inc(), 1)
        on("gpsimd", _gq)

        def _v3(v):
            v.wait_ge(GG.h, GG.v)
            if dst_waits:
                for sh, sv_ in dst_waits:
                    v.wait_ge(sh, sv_)
            v._custom_dve(op1, out=Av, in0=wv, in1=c3col[:],
                          s0=_T1, s1=_T2, imm2=_T3)
            v.drain()
            v._custom_dve(op2, out=Hv, in0=wv, in1=Av, s0=_T5, s1=_T6, imm2=_T7)
            hm_b = hm.unsqueeze(2).broadcast_to([128, nb, 16])
            v.scalar_tensor_tensor(out=Mv.rearrange("p (nb k) -> p nb k", k=16),
                                   in0=sg.rearrange("p (nb k) -> p nb k", k=16),
                                   scalar=-1.0, in1=hm_b, op0=ALU.add, op1=ALU.mult)
            v.drain()
            _li = v.tensor_tensor(out=dst_ap, in0=Hv, in1=Mv, op=ALU.mult)
            _li.then_inc(SV.inc(), 1)
        on("vector", _v3)

    ffn_tiles = ([("g", t, 0) for t in range(MT)] +
                 [("u", t, 0) for t in range(MT)] +
                 [("d", t, c) for t in range(KT) for c in range(3)])

    def emit_ffn_wtile(kind, t, c, slot):
        if kind == "g":
            src, dsth, W, c0 = wg_h[t * 128:(t + 1) * 128, :], wgT_h, D, 0
        elif kind == "u":
            src, dsth, W, c0 = wu_h[t * 128:(t + 1) * 128, :], wuT_h, D, 0
        else:
            c0 = c * 1024
            W = min(1024, I - c0)
            src = wd_h[t * 128:(t + 1) * 128, c0:c0 + W]
            dsth = wdT_h
        nch = W // 128

        def _sy(sy):
            sy.wait_ge(SV.h, ffn_guard["sv"])
            sy.wait_ge(SS.h, ffn_guard["ss"])
            sy.dma_start(out=fw32[:, slot, 0:W], in_=src).then_inc(QD.inc(16), 16)
        on("sync", _sy)
        qd_t = QD.v

        emit_quant(slot, W, fwq[:, slot, 0:W], src_ap=fw32[:, slot, 0:W],
                   dst_waits=[(TP.h, TP.v)], src_waits=[(QD.h, qd_t)])
        sv_q = SV.v
        ffn_guard["sv"] = SV.v
        ffn_guard["ss"] = SS.v

        def _sy2(sy, sv_q=sv_q, nch=nch, kind=kind, t=t, c0=c0, dsth=dsth, W=W,
                 slot=slot, stw=None):
            sy.wait_ge(SV.h, sv_q)
            sy.wait_ge(ST.h, ST.v)
            sy.dma_start_transpose(out=fwT[:, 0:nch, :], in_=fwq[:, slot, 0:W]
                                   ).then_inc(TP.inc(16), 16)
            sy.wait_ge(TP.h, TP.v)
            dst = dsth.rearrange("c p n -> p c n")[:, c0 // 128:c0 // 128 + nch,
                                                   t * 128:(t + 1) * 128]
            sy.dma_start(out=dst, in_=fwT[:, 0:nch, :]).then_inc(ST.inc(16), 16)
        on("sync", _sy2)
        ffn_guard["tp"] = TP.v

    sc_ps = [psC[:, 0:384], psC[:, 512:896]]
    o65_ps = [psD[:, 0:65], psD[:, 512:577]]
    NLP = min(NL, 64)
    lsc_ps = [psB[0:NLP, 0:128], psB[0:NLP, 512:640]]
    r65_ps = [psA[:, 0:65], psA[:, 512:577]]

    prev_tp_xnsb = [0]
    prev_st_xqsb = [0]
    prev_st_mixed = [0]
    ffn_guard = {"sv": 0, "ss": 0, "tp": 0}
    xin_free = {0: 0, 1: 0}
    xnsb_free = {0: 0, 1: 0}
    xq_free = {0: 0, 1: 0}
    for sbi in range(NSB):
        t0 = sbi * 1024
        last = (sbi == NSB - 1)
        org = 8 * sbi - 4
        gorg = 8 * sbi - 2

        tpa = {}
        sta = {}
        SA0 = SS.v
        sv_var = {}
        for st in range(8):
            slot = st % 2
            tok = t0 + st * 128

            def _sy(sy, slot=slot, tok=tok):
                sy.wait_ge(SV.h, xin_free[slot])
                sy.dma_start(out=xin[:, slot, :], in_=x_h[tok:tok + 128, :]
                             ).then_inc(QD.inc(16), 16)
            on("sync", _sy)
            qd_tile = QD.v

            def _v(v, slot=slot, qd=qd_tile):
                v.wait_ge(QD.h, qd)
                xi = xin[:, slot, :]
                v.bn_stats(out=stat[:, slot, 0:6], in_=xi[:, 0:512])
                v.bn_stats(out=stat[:, slot, 6:12], in_=xi[:, 512:1024])
                v.drain()
                v.bn_aggr(out=aggr[:, slot, :], in_=stat[:, slot, :])
                v.drain()
                _li = v.tensor_scalar(out=lncol[:, 4 * slot:4 * slot + 1],
                                      in0=aggr[:, slot, 1:2], scalar1=1e-5,
                                      scalar2=None, op0=ALU.add)
                _li.then_inc(SV.inc(), 1)
            on("vector", _v)
            sv_var[st] = SV.v

            rstd = lncol[:, 4 * slot + 2:4 * slot + 3]

            def _v2b(v, slot=slot, rstd=rstd, ggw=GG.v, st=st, SA0=SA0):
                v.wait_ge(SS.h, SA0 + st + 1)
                v.wait_ge(GG.h, ggw)
                v.reciprocal(out=rstd, in_=lncol[:, 4 * slot + 1:4 * slot + 2])
                v.drain()
                _li = v.scalar_tensor_tensor(out=qw[:, 0, :], in0=xin[:, slot, :],
                                             scalar=aggr[:, slot, 0:1],
                                             in1=rstd.broadcast_to([128, D]),
                                             op0=ALU.subtract, op1=ALU.mult)
                _li.then_inc(SV.inc(), 1)
            on("vector", _v2b)
            sv_xn = SV.v
            xin_free[slot] = SV.v

            def _gxn(g, slot=slot, sv_xn=sv_xn, tp_guard=xnsb_free[slot]):
                g.wait_ge(SV.h, sv_xn)
                g.wait_ge(TP.h, tp_guard)
                g.tensor_copy(out=xn_sb[:, slot, :], in_=qw[:, 0, :]
                              ).then_inc(GG.inc(), 1)
            on("gpsimd", _gxn)
            gg_xn = GG.v

            def _sy2(sy, st=st, slot=slot, gg_xn=gg_xn):
                sy.wait_ge(GG.h, gg_xn)
                sy.wait_ge(TP.h, xnsb_free[1 - slot])
                sy.dma_start_transpose(out=xn_fm[:, :, st * 128:(st + 1) * 128],
                                       in_=xn_sb[:, slot, :]).then_inc(TP.inc(16), 16)
            on("sync", _sy2)
            tpa[st] = TP.v
            xnsb_free[slot] = TP.v

            emit_quant(slot, D, xq_sb[:, slot, :], src_ap=qw[:, 0, :],
                       dst_waits=[(ST.h, ST.v)], lead_drain=True)
            sv_xq = SV.v

            def _sy3(sy, tok=tok, slot=slot, sv_xq=sv_xq):
                sy.wait_ge(SV.h, sv_xq)
                sy.dma_start(out=xq_h[tok:tok + 128, :], in_=xq_sb[:, slot, :]
                             ).then_inc(ST.inc(16), 16)
            on("sync", _sy3)
            sta[st] = ST.v
            xq_free[slot] = ST.v

        def _sqA(s, waits=dict(sv_var), SA0=SA0):
            for st in range(8):
                slot = st % 2
                s.wait_ge(SV.h, waits[st])
                s.activation(out=lncol[:, 4 * slot + 1:4 * slot + 2],
                             in_=lncol[:, 4 * slot:4 * slot + 1],
                             func=AF.Sqrt).then_inc(SS.inc(), 1)
        on("scalar", _sqA)
        assert SS.v == SA0 + 8

        tp_xn = TP.v
        prev_tp_xnsb[0] = tpa[7]
        prev_st_xqsb[0] = sta[7]

        n_all = len(ffn_tiles)
        per_sb = (n_all + NSB - 1) // NSB
        for wi in range(per_sb):
            idx = sbi * per_sb + wi
            if idx < n_all:
                kind, t, c = ffn_tiles[idx]
                emit_ffn_wtile(kind, t, c, 0)

        panels = [(p, d) for p in range(3) for d in range(KT)] + \
                 [(3, d) for d in range(LT)]
        tt_done = {}
        SSB0 = SS.v
        TTB0 = TT.v
        vtp_rec = {}
        for pi, (proj, dt_) in enumerate(panels):
            slot = pi % 2
            srcw = [wq_h, wk_h, wv_h, wgq_h][proj]

            wp = WP[slot]

            def _g(g, slot=slot, srcw=srcw, dt_=dt_, pi=pi, wp=wp):
                if pi >= 2:
                    g.wait_ge(TT.h, tt_done[pi - 2])
                g.dma_start(out=wpan[:, slot, :, :],
                            in_=srcw[:, dt_ * 128:(dt_ + 1) * 128]
                            .rearrange("(c p) n -> p c n", p=128)
                            ).then_inc(wp.inc(16), 16)
            on("gpsimd", _g)
            gd_p = wp.v
            ps = psA if pi % 2 == 0 else psB
            evac_prev = SSB0 + pi - 1 if pi >= 2 else SSB0

            wph = wp.h

            def _t(t, slot=slot, gd_p=gd_p, ps=ps, tp_xn=tp_xn, evac_prev=evac_prev,
                   wph=wph):
                t.wait_ge(wph, gd_p)
                t.wait_ge(TP.h, tp_xn)
                t.wait_ge(SS.h, evac_prev)
                for k in range(KT):
                    t.matmul(ps[:, 0:512], wpan[:, slot, k, :], xn_fm[:, k, 0:512],
                             start=(k == 0), stop=(k == KT - 1))
                    _li = t.matmul(ps[:, 512:1024], wpan[:, slot, k, :],
                             xn_fm[:, k, 512:1024],
                             start=(k == 0), stop=(k == KT - 1))
                _li.then_inc(TT.inc(), 1)
            on("tensor", _t)
            tt_done[pi] = TT.v
            vtp_rec[pi] = TP.v

            if proj == 2:
                def _sy(sy, dt_=dt_, ss_p=SSB0 + pi + 1, svw=SV.v):
                    sy.wait_ge(SS.h, ss_p)
                    sy.wait_ge(SV.h, svw)
                    sy.dma_start_transpose(out=vtsc[:], in_=vtmp[:]
                                           ).then_inc(TP.inc(16), 16)
                on("sync", _sy)

                def _vv(v, dt_=dt_, tpw=TP.v):
                    v.wait_ge(TP.h, tpw)
                    for j in range(2):
                        h = 2 * dt_ + j
                        v.tensor_copy(out=v65[:, 4:12, 65 * h:65 * h + 64],
                                      in_=vtsc[:, :, 64 * j:64 * (j + 1)])
                    v.drain().then_inc(SV.inc(), 1)
                on("vector", _vv)

        def _sbev(s, panels=panels, TTB0=TTB0, vt=dict(vtp_rec)):
            for pi, (proj, dt_) in enumerate(panels):
                ps = psA if pi % 2 == 0 else psB
                s.wait_ge(TT.h, TTB0 + pi + 1)
                if proj == 2:
                    s.wait_ge(TP.h, vt[pi])
                if proj == 0:
                    _li = s.activation(out=qbuf[:, dt_, 512:1536], in_=ps[:],
                                       func=AF.Identity)
                elif proj == 1:
                    _li = s.activation(out=kbuf[:, dt_, 512:1536], in_=ps[:],
                                       func=AF.Identity)
                elif proj == 2:
                    _li = s.activation(out=vtmp[:], in_=ps[:], func=AF.Identity)
                else:
                    _li = s.activation(out=gqbuf[:, dt_, 256:1280], in_=ps[:],
                                       func=AF.Identity)
                _li.then_inc(SS.inc(), 1)
        on("scalar", _sbev)
        assert SS.v == SSB0 + len(panels)

        tp_v65 = TP.v

        def _vp(v, tp_v65=tp_v65):
            v.wait_ge(TP.h, tp_v65)
            v.tensor_reduce(out=pooledf[:],
                            in_=xn_fm[:].rearrange("p c (l w) -> p (c l) w", w=64)
                            .rearrange("p (c l) w -> p c l w", c=KT),
                            axis=AX.X, op=ALU.add)
            v.drain()
            v.tensor_scalar(out=pooled[:], in0=pooledf[:], scalar1=1.0 / 64.0,
                            scalar2=None, op0=ALU.mult)
            v.drain().then_inc(SV.inc(), 1)
        on("vector", _vp)
        sv_pool = SV.v
        l0 = sbi * 16
        SSC0 = SS.v
        TTC0 = TT.v

        for gt in range(LT):
            def _g(g, gt=gt):
                g.wait_ge(TT.h, TT.v)
                g.dma_start(out=wpan[:, 0, :, :],
                            in_=wgk_h[:, gt * 128:(gt + 1) * 128]
                            .rearrange("(c p) n -> p c n", p=128)).then_inc(GD.inc(16), 16)
                g.dma_start(out=wpan[:, 1, :, :],
                            in_=wgv_h[:, gt * 128:(gt + 1) * 128]
                            .rearrange("(c p) n -> p c n", p=128)).then_inc(GD.inc(16), 16)
            on("gpsimd", _g)
            gd_g = GD.v

            def _t(t, gt=gt, gd_g=gd_g, sv_pool=sv_pool, ssw=SSC0 + 2 * gt):
                t.wait_ge(GD.h, gd_g)
                t.wait_ge(SV.h, sv_pool)
                t.wait_ge(SS.h, ssw)
                for k in range(KT):
                    t.matmul(psA[:, 0:16], wpan[:, 0, k, :], pooled[:, k, :],
                             start=(k == 0), stop=(k == KT - 1))
                    _li = t.matmul(psB[:, 0:16], wpan[:, 1, k, :], pooled[:, k, :],
                             start=(k == 0), stop=(k == KT - 1))
                _li.then_inc(TT.inc(), 1)
            on("tensor", _t)

            def _tv(t, gt=gt, ss_g=SSC0 + 2 * gt + 1):
                t.wait_ge(SS.h, ss_g)
                _li = t.transpose(psC[0:NLP, 0:128], gvfm[:, gt, 0:NL], idmat[:])
                _li.then_inc(TT.inc(), 1)
            on("tensor", _tv)

        def _tf(t, ssw=SSC0 + 2 * LT):
            t.wait_ge(SS.h, ssw)
            for f in range(KT):
                t.matmul(psA[0:16, 0:512], gpad[:, f, 0:16],
                         qbuf[:, f, 512:1024], start=(f == 0), stop=(f == KT - 1))
                t.matmul(psA[0:16, 512:1024], gpad[:, f, 0:16],
                         qbuf[:, f, 1024:1536], start=(f == 0), stop=(f == KT - 1))
                t.matmul(psB[0:16, 0:512], gpad[:, f, 16:32],
                         qbuf[:, f, 512:1024], start=(f == 0), stop=(f == KT - 1))
                _li = t.matmul(psB[0:16, 512:1024], gpad[:, f, 16:32],
                               qbuf[:, f, 1024:1536], start=(f == 0),
                               stop=(f == KT - 1))
            _li.then_inc(TT.inc(), 1)
        on("tensor", _tf)
        tt_f = TT.v

        def _scs(s, tt_f=tt_f, svw=SV.v, l0=l0, SSC0=SSC0, TTC0=TTC0):
            for gt in range(LT):
                s.wait_ge(TT.h, TTC0 + 2 * gt + 1)
                s.activation(out=gkfm[:, gt, l0:l0 + 16], in_=psA[:, 0:16],
                             func=AF.Identity)
                _li = s.activation(out=gvfm[:, gt, l0:l0 + 16], in_=psB[:, 0:16],
                                   func=AF.Identity)
                _li.then_inc(SS.inc(), 1)
                s.wait_ge(TT.h, TTC0 + 2 * gt + 2)
                s.activation(out=gv65[0:NL, 2 * gt, 0:64],
                             in_=psC[0:NLP, 0:64], func=AF.Identity)
                _li = s.activation(out=gv65[0:NL, 2 * gt + 1, 0:64],
                                   in_=psC[0:NLP, 64:128], func=AF.Identity)
                _li.then_inc(SS.inc(), 1)
            s.wait_ge(TT.h, tt_f)
            s.wait_ge(SV.h, svw)
            s.activation(out=glps[:, 0, :], in_=psA[0:16, :], func=AF.Identity)
            _li = s.activation(out=glps[:, 1, :], in_=psB[0:16, :],
                               func=AF.Identity)
            _li.then_inc(SS.inc(), 1)
            s.drain()
            s.activation(out=glps[:].rearrange("h a n -> h (a n)"),
                         in_=glps[:].rearrange("h a n -> h (a n)"),
                         func=AF.Exp, scale=-1.0).then_inc(SS.inc(), 1)
        on("scalar", _scs)
        assert SS.v == SSC0 + 2 * LT + 2

        ss_gexp = SS.v

        def _syg(sy, ss_gexp=ss_gexp):
            sy.wait_ge(SS.h, ss_gexp)
            sy.dma_start_transpose(out=glcb[:, 0, 2:10, :], in_=glps[:, 0, :]
                                   ).then_inc(TP.inc(16), 16)
            sy.dma_start_transpose(out=glcb[:, 1, 2:10, :], in_=glps[:, 1, :]
                                   ).then_inc(TP.inc(16), 16)
        on("sync", _syg)
        tp_glt = TP.v

        def _vgc(v, tp_glt=tp_glt):
            v.wait_ge(TP.h, tp_glt)
            v.tensor_scalar(out=glcols[:, 2:10, :], in0=glcb[:, 0, 2:10, :],
                            scalar1=1.0, scalar2=None, op0=ALU.add)
            v.tensor_scalar(out=grcols[:, 2:10, :], in0=glcb[:, 1, 2:10, :],
                            scalar1=1.0, scalar2=None, op0=ALU.add)
            v.drain()
            v.reciprocal(out=glcols[:, 2:10, :], in_=glcols[:, 2:10, :])
            _li = v.reciprocal(out=grcols[:, 2:10, :], in_=grcols[:, 2:10, :])
            _li.then_inc(SV.inc(), 1)
        on("vector", _vgc)
        sv_glc = SV.v

        c_lo = max(0, 8 * sbi - 4)
        c_hi = (8 * sbi + 5) if not last else (NG - 1)
        sweep = list(range(c_lo, c_hi + 1))
        nsw = len(sweep)
        pv_lo = max(0, gorg)
        g_lo = pv_lo
        g_hi = c_hi
        ngr = g_hi - g_lo + 1

        def dblock(h):
            tev, sev, vev = [], [], []
            for ci in range(nsw):
                tev.append(("sc", h, ci))
                if ci >= 1 and sweep[ci - 1] >= pv_lo:
                    tev.append(("pv", h, ci - 1))
                sev.append(("exp", h, ci))
                vev.append(("mask", h, ci))
                if ci >= 1 and sweep[ci - 1] >= pv_lo:
                    vev.append(("evac", h, ci - 1))
            if sweep[nsw - 1] >= pv_lo:
                tev.append(("pv", h, nsw - 1))
                vev.append(("evac", h, nsw - 1))
            return tev, sev, vev

        def eblock(gi):
            tev, sev, vev = [], [], []
            for lh in range(LH):
                tev.append(("lsc", gi, lh))
                if lh >= 1:
                    tev.append(("rem", gi, lh - 1))
                sev.append(("lexp", gi, lh))
                vev.append(("lmask", gi, lh))
                if lh >= 1:
                    vev.append(("levac", gi, lh - 1))
            tev.append(("rem", gi, LH - 1))
            vev.append(("levac", gi, LH - 1))
            return tev, sev, vev

        T_ord, S_ord, V_ord = [], [], []
        for h in range(H):
            a, b, c = dblock(h)
            T_ord += a; S_ord += b; V_ord += c
            if h % 2 == 1 and h // 2 < ngr:
                a, b, c = eblock(h // 2)
                T_ord += a; S_ord += b; V_ord += c
        for gi in range(H // 2, ngr):
            a, b, c = eblock(gi)
            T_ord += a; S_ord += b; V_ord += c

        TT0, SS0, SV0 = TT.v, SS.v, SV.v
        tt_n, ss_n, sv_n = {}, {}, {}
        cnt = TT0
        for ev in T_ord:
            cnt += 1
            tt_n[ev] = cnt
        tt_end = cnt
        cnt = SS0
        for ev in S_ord:
            cnt += 1
            ss_n[ev] = cnt
        ss_end = cnt
        cnt = SV0
        for ev in V_ord:
            cnt += 1
            sv_n[ev] = cnt
        sv_end = cnt

        def _bigt(t, T_ord=T_ord, tt_n=tt_n, ss_n=ss_n, sv_n=sv_n, SS0=SS0,
                  sweep=sweep, nsw=nsw, pv_lo=pv_lo, org=org):
            for ev in T_ord:
                kind = ev[0]
                if kind == "sc":
                    _, h, ci = ev
                    C = sweep[ci]
                    dt_, half = h // 2, h % 2
                    kslot = C - org
                    N = min(384, S - 128 * C)
                    if ci >= 2:
                        pw = ss_n[("exp", h, ci - 2)]
                    elif h >= 1:
                        pw = ss_n[("exp", h - 1, ci - 2 + nsw)]
                    else:
                        pw = SS0
                    t.wait_ge(SS.h, pw)
                    _li = t.matmul(sc_ps[ci % 2][:, 0:N],
                                   kbuf[64 * half:64 * (half + 1), dt_,
                                        128 * kslot:128 * kslot + 128],
                                   qbuf[64 * half:64 * (half + 1), dt_,
                                        128 * kslot:128 * kslot + N],
                                   start=True, stop=True)
                elif kind == "pv":
                    _, h, ci = ev
                    g = sweep[ci]
                    t.wait_ge(SV.h, sv_n[("mask", h, ci)])
                    chunks = list(range(max(0, g - 2), g + 1))
                    for j, cc in enumerate(chunks):
                        _li = t.matmul(o65_ps[ci % 2],
                                       ebuf[:, cc % 4 + 4 * (h % 2),
                                            128 * (g - cc):128 * (g - cc) + 128],
                                       v65[:, cc - org, 65 * h:65 * (h + 1)],
                                       start=(j == 0),
                                       stop=(j == len(chunks) - 1))
                elif kind == "lsc":
                    _, gi, lh = ev
                    k = gi * LH + lh
                    ldt, lhalf = lh // 2, lh % 2
                    qoff = 128 * ((g_lo + gi) - gorg)
                    if lh >= 2:
                        pw = ss_n[("lexp", gi, lh - 2)]
                    elif gi >= 1:
                        pw = ss_n[("lexp", gi - 1, lh + LH - 2)]
                    else:
                        pw = SS0
                    t.wait_ge(SS.h, pw)
                    _li = t.matmul(lsc_ps[k % 2],
                                   gkfm[64 * lhalf:64 * lhalf + 64, ldt, 0:NL],
                                   gqbuf[64 * lhalf:64 * (lhalf + 1), ldt,
                                         qoff:qoff + 128],
                                   start=True, stop=True)
                else:
                    _, gi, lh = ev
                    k = gi * LH + lh
                    t.wait_ge(SV.h, sv_n[("lmask", gi, lh)])
                    _li = t.matmul(r65_ps[k % 2], lexps[0:NL, k % 2, :],
                                   gv65[0:NL, lh, :], start=True, stop=True)
                _li.then_inc(TT.inc(), 1)
        on("tensor", _bigt)
        assert TT.v == tt_end

        def _bigs(s, S_ord=S_ord, tt_n=tt_n, sweep=sweep, nsw=nsw, pv_lo=pv_lo):
            for ev in S_ord:
                if ev[0] == "exp":
                    _, h, ci = ev
                    C = sweep[ci]
                    N = min(384, S - 128 * C)
                    w = tt_n[("sc", h, ci)]
                    prev_pvs = [cj for cj in range(ci - 1)
                                if sweep[cj] >= pv_lo]
                    if prev_pvs:
                        w = max(w, tt_n[("pv", h, prev_pvs[-1])])
                    elif h >= 2:
                        w = max(w, tt_n[("pv", h - 2, nsw - 1)])
                    s.wait_ge(TT.h, w)
                    _li = s.activation(out=ebuf[:, C % 4 + 4 * (h % 2), 0:N],
                                       in_=sc_ps[ci % 2][:, 0:N],
                                       func=AF.Exp, scale=0.125)
                else:
                    _, gi, lh = ev
                    k = gi * LH + lh
                    w = tt_n[("lsc", gi, lh)]
                    if lh >= 2:
                        w = max(w, tt_n[("rem", gi, lh - 2)])
                    elif gi >= 1:
                        w = max(w, tt_n[("rem", gi - 1, lh + LH - 2)])
                    s.wait_ge(TT.h, w)
                    _li = s.activation(out=lexps[0:NL, k % 2, :],
                                       in_=lsc_ps[k % 2],
                                       func=AF.Exp, scale=0.125)
                _li.then_inc(SS.inc(), 1)
        on("scalar", _bigs)
        assert SS.v == ss_end

        stmx = prev_st_mixed[0]

        def _bigv(v, V_ord=V_ord, tt_n=tt_n, ss_n=ss_n, sweep=sweep,
                  stmx=stmx, gorg=gorg, g_lo=g_lo):
            v.wait_ge(ST.h, stmx)
            for ev in V_ord:
                kind = ev[0]
                if kind == "mask":
                    _, h, ci = ev
                    C = sweep[ci]
                    N = min(384, S - 128 * C)
                    es_ = C % 4 + 4 * (h % 2)
                    v.wait_ge(SS.h, ss_n[("exp", h, ci)])
                    _li = v.tensor_tensor(out=ebuf[:, es_, 0:N],
                                          in0=ebuf[:, es_, 0:N],
                                          in1=mask_loc[:, 0:N], op=ALU.mult)
                elif kind == "evac":
                    _, h, ci = ev
                    g = sweep[ci]
                    gsl = g - gorg
                    par2 = ci % 2
                    v.wait_ge(TT.h, tt_n[("pv", h, ci)])
                    v.reciprocal(out=fcol[:, par2:par2 + 1],
                                 in_=o65_ps[par2][:, 64:65])
                    v.drain()
                    _li = v.scalar_tensor_tensor(
                        out=mixed_tm[:, gsl, 64 * h:64 * (h + 1)],
                        in0=o65_ps[par2][:, 0:64],
                        scalar=fcol[:, par2:par2 + 1],
                        in1=glcols[:, gsl, h:h + 1].broadcast_to([128, 64]),
                        op0=ALU.mult, op1=ALU.mult)
                elif kind == "lmask":
                    _, gi, lh = ev
                    k = gi * LH + lh
                    g = g_lo + gi
                    v.wait_ge(SS.h, ss_n[("lexp", gi, lh)])
                    _li = v.tensor_mask(out=lexps[0:NL, k % 2, :],
                                        in_=lexps[0:NL, k % 2, :],
                                        mask_length=latthr[0:NL, g:g + 1],
                                        mask_offsets=latnegq[0:NL, :], offset=0)
                else:
                    _, gi, lh = ev
                    k = gi * LH + lh
                    gsl = (g_lo + gi) - gorg
                    v.wait_ge(TT.h, tt_n[("rem", gi, lh)])
                    v.reciprocal(out=fcol[:, 2 + k % 2:3 + k % 2],
                                 in_=r65_ps[k % 2][:, 64:65])
                    v.drain()
                    _li = v.tensor_tensor(
                        out=remp_tm[:, gsl, 64 * lh:64 * (lh + 1)],
                        in0=r65_ps[k % 2][:, 0:64],
                        in1=fcol[:, 2 + k % 2:3 + k % 2].broadcast_to([128, 64]),
                        op=ALU.mult)
                _li.then_inc(SV.inc(), 1)
        on("vector", _bigv)
        assert SV.v == sv_end

        ss_f = {gi: ss_end + gi + 1 for gi in range(ngr)}
        sv_fv = {gi: sv_end + gi + 1 for gi in range(ngr)}
        tt_r = {}
        for gi, g in enumerate(range(g_lo, g_hi + 1)):
            gsl = g - gorg
            tt_prev = tt_r.get(gi - 1, 0)

            def _syf(sy, gsl=gsl, w=sv_n[("levac", gi, LH - 1)], tt_prev=tt_prev):
                sy.wait_ge(SV.h, w)
                if tt_prev:
                    sy.wait_ge(TT.h, tt_prev)
                sy.dma_start_transpose(out=remp_fm[:, :, :],
                                       in_=remp_tm[:, gsl, :]).then_inc(TP.inc(16), 16)
            on("sync", _syf)
            tp_r = TP.v

            def _tf2(t, tp_r=tp_r, gi=gi, sv_end=sv_end):
                t.wait_ge(TP.h, tp_r)
                if gi == 0:
                    t.wait_ge(SV.h, sv_end)
                else:
                    t.wait_ge(SS.h, ss_f[gi - 1])
                for lt in range(LT):
                    t.matmul(psA[:, 0:512], remp_fm[:, lt, :], routw[:, lt, 0:512],
                             start=(lt == 0), stop=(lt == LT - 1))
                    _li = t.matmul(psA[:, 512:1024], remp_fm[:, lt, :],
                                   routw[:, lt, 512:1024],
                                   start=(lt == 0), stop=(lt == LT - 1))
                _li.then_inc(TT.inc(), 1)
            on("tensor", _tf2)
            tt_r[gi] = TT.v

        def _fs(s, items=[(gi, g) for gi, g in enumerate(range(g_lo, g_hi + 1))],
                tt_r=dict(tt_r), sv_fv=sv_fv, gorg=gorg):
            for gi, g in items:
                gsl = g - gorg
                s.wait_ge(TT.h, tt_r[gi])
                if gi >= 1:
                    s.wait_ge(SV.h, sv_fv[gi - 1])
                for h in range(H):
                    _li = s.activation(out=remg[:, 64 * h:64 * (h + 1)],
                                       in_=psA[:, 64 * h:64 * (h + 1)],
                                       func=AF.Identity,
                                       scale=grcols[:, gsl, h:h + 1])
                _li.then_inc(SS.inc(), 1)
        on("scalar", _fs)
        assert SS.v == ss_end + ngr

        def _fv(v, items=[(gi, g) for gi, g in enumerate(range(g_lo, g_hi + 1))],
                ss_f=ss_f, gorg=gorg):
            for gi, g in items:
                gsl = g - gorg
                v.wait_ge(SS.h, ss_f[gi])
                _li = v.tensor_tensor(out=mixed_tm[:, gsl, :],
                                      in0=mixed_tm[:, gsl, :],
                                      in1=remg[:], op=ALU.add)
                _li.then_inc(SV.inc(), 1)
        on("vector", _fv)
        assert SV.v == sv_end + ngr

        sv_f = SV.v
        sl_lo = g_lo - gorg
        n_sl = g_hi - g_lo + 1

        def _sy(sy, sv_f=sv_f, sl_lo=sl_lo, n_sl=n_sl, g_lo=g_lo):
            sy.wait_ge(SV.h, sv_f)
            sy.dma_start(out=mixed_h[128 * g_lo:128 * (g_lo + n_sl), :]
                         .rearrange("(a p) d -> p a d", p=128),
                         in_=mixed_tm[:, sl_lo:sl_lo + n_sl, :]
                         ).then_inc(ST.inc(16), 16)
        on("sync", _sy)
        prev_st_mixed[0] = ST.v

        if not last:
            def _vt(v):
                v.wait_ge(SV.h, SV.v)
                v.wait_ge(SS.h, SS.v)
                v.tensor_copy(out=kbuf[:, :, 0:512], in_=kbuf[:, :, 1024:1536])
                v.tensor_copy(out=qbuf[:, :, 0:512], in_=qbuf[:, :, 1024:1536])
                v.tensor_copy(out=gqbuf[:, :, 0:256], in_=gqbuf[:, :, 1024:1280])
                v.tensor_copy(out=v65[:, 0:4, :], in_=v65[:, 8:12, :])
                v.tensor_copy(out=glcols[:, 0:2, :], in_=glcols[:, 8:10, :])
                _li = v.tensor_copy(out=grcols[:, 0:2, :], in_=grcols[:, 8:10, :])
                _li.then_inc(SV.inc(), 1)
            on("vector", _vt)
            sv_tail = SV.v

            def _tb(t, sv_tail=sv_tail):
                t.wait_ge(SV.h, sv_tail)
            on("tensor", _tb)

    if debug == 2:
        dbg_names = [("qbuf", qbuf), ("kbuf", kbuf), ("v65", v65),
                     ("glcols", glcols), ("grcols", grcols), ("gv65", gv65),
                     ("gkfm", gkfm), ("gqbuf", gqbuf), ("ebuf", ebuf),
                     ("remp", remp_tm), ("mixtm", mixed_tm), ("fcold", fcol)]
        dbg_h = {}
        for nm, tsr in dbg_names:
            ap = tsr if not hasattr(tsr, "ap") else tsr.ap()
            shape = list(ap.shape) if hasattr(ap, "shape") else list(tsr.shape)
            dt_ = ap.dtype
            dbg_h[nm] = nc.dram_tensor("dbg_" + nm, shape, dt_,
                                       kind="ExternalOutput").ap()

        def _dmp(sy):
            sy.wait_ge(SV.h, SV.v)
            sy.wait_ge(SS.h, SS.v)
            sy.wait_ge(TT.h, TT.v)
            for nm, tsr in dbg_names:
                ap = tsr if not hasattr(tsr, "ap") else tsr.ap()
                sy.dma_start(out=dbg_h[nm][:], in_=ap[:] if hasattr(ap, "shape") else ap
                             ).then_inc(ST.inc(16), 16)
            sy.wait_ge(ST.h, ST.v)
        on("sync", _dmp)
        es1.close()
        es.close()
        return nc

    es1.close()

    p1_sv, p1_ss, p1_tt, p1_gg = SV.v, SS.v, TT.v, GG.v
    p1_tp, p1_st, p1_qd, p1_gd = TP.v, ST.v, QD.v, GD.v

    es3 = contextlib.ExitStack()
    def sb3(name, shape, dt):
        return es3.enter_context(nc.sbuf_tensor(name, shape, dt))
    wgT = sb3("wgT", [128, KT, I], BF16)
    wuT = sb3("wuT", [128, KT, I], BF16)
    wdT = sb3("wdT", [128, MT, D], BF16)
    woutw = sb3("woutw", [128, KT, D], BF16)
    xqr = sb3("xqr", [128, 2, D], BF16)
    xqf = sb3("xqf", [128, 2, KT, 128], BF16)
    mxr = sb3("mxr", [128, 2, D], BF16)
    mxf = sb3("mxf", [128, 2, KT, 128], BF16)
    hbuf = sb3("hbuf", [128, D], F32)
    hq = sb3("hq", [128, MT, 128], BF16)
    hqt = sb3("hqt", [128, I], BF16)
    xres = sb3("xres", [128, 2, D], F32)

    def _barrier(e):
        e.wait_ge(SV.h, p1_sv)
        e.wait_ge(SS.h, p1_ss)
        e.wait_ge(TT.h, p1_tt)
        e.wait_ge(GG.h, p1_gg)
        e.wait_ge(TP.h, p1_tp)
        e.wait_ge(ST.h, p1_st)
        e.wait_ge(QD.h, p1_qd)
        e.wait_ge(GD.h, p1_gd)

    def _syl(sy):
        _barrier(sy)
        for k in range(KT):
            sy.dma_start(out=wgT[:, k, :], in_=wgT_h[k]).then_inc(QD.inc(16), 16)
            sy.dma_start(out=wuT[:, k, :], in_=wuT_h[k]).then_inc(QD.inc(16), 16)
        for m in range(MT):
            sy.dma_start(out=wdT[:, m, :], in_=wdT_h[m]).then_inc(QD.inc(16), 16)
        sy.wait_ge(QD.h, QD.v)
    on("sync", _syl)

    def _gl(g):
        _barrier(g)
        for k in range(KT):
            g.dma_start(out=woutw[:, k, :], in_=wout_h[k * 128:(k + 1) * 128, :]
                        ).then_inc(GD.inc(16), 16)
        g.wait_ge(GD.h, GD.v)
    on("gpsimd", _gl)
    p3_qd = QD.v
    p3_gd = GD.v

    for eng in ["tensor", "vector", "scalar"]:
        def _b(e):
            _barrier(e)
        on(eng, _b)

    ngs = [(0, 1024), (1024, 2048), (2048, 2816)]
    NT = S // 128
    SS3 = SS.v
    SVB, GGB = SV.v, GG.v
    R = {}

    def gg_cf(st, ci):
        return GGB + 3 * st + ci + 1

    def emit_A(st):
        slot = st % 2
        tok = st * 128

        def _sy(sy, slot=slot, tok=tok, st=st):
            if st >= 2:
                sy.wait_ge(TP.h, R[st - 2, "tp_hq"])
            elif st == 1:
                sy.wait_ge(TP.h, R[0, "tp_pair"])
            if st >= 2:
                sy.wait_ge(ST.h, R[st - 2, "st_store"])
                sy.wait_ge(TT.h, R[st - 2, "tt_out"])
            sy.dma_start(out=xqr[:, slot, :], in_=xq_h[tok:tok + 128, :]
                         ).then_inc(QD.inc(16), 16)
            sy.dma_start(out=mxr[:, slot, :], in_=mixed_h[tok:tok + 128, :]
                         ).then_inc(QD.inc(16), 16)
            sy.dma_start(out=xres[:, slot, :], in_=x_h[tok:tok + 128, :]
                         ).then_inc(QD.inc(16), 16)
            sy.wait_ge(QD.h, QD.v)
            sy.dma_start_transpose(out=xqf[:, slot, :, :], in_=xqr[:, slot, :]
                                   ).then_inc(TP.inc(16), 16)
            sy.dma_start_transpose(out=mxf[:, slot, :, :], in_=mxr[:, slot, :]
                                   ).then_inc(TP.inc(16), 16)
        on("sync", _sy)
        R[st, "qd"] = QD.v
        R[st, "tp_pair"] = TP.v

    def emit_H(st):
        def _sy2(sy, st=st):
            sy.wait_ge(SV.h, R[st, "sv_v3_2"])
            sy.wait_ge(TP.h, R.get((st + 1, "tp_pair"),
                                   R[st, "tp_pair"]))
            if st >= 1:
                sy.wait_ge(TT.h, R[st - 1, "tt_out"])
            sy.dma_start_transpose(out=hq[:, :, :], in_=hqt[:]
                                   ).then_inc(TP.inc(16), 16)
        on("sync", _sy2)
        R[st, "tp_hq"] = TP.v

    def emit_tile_chunks(st):
        slot = st % 2
        for ci, (n0, n1) in enumerate(ngs):
            W = n1 - n0
            ssw = SS3 + 3 * st + ci
            if ci == 0:
                svw = R[st - 1, "sv_hm_2"] if st >= 1 else 0
            else:
                svw = R[st, "sv_hm_%d" % (ci - 1)]

            def _t(t, slot=slot, n0=n0, W=W, st=st, ssw=ssw, svw=svw):
                t.wait_ge(TP.h, R[st, "tp_pair"])
                t.wait_ge(QD.h, p3_qd)
                t.wait_ge(SS.h, ssw)
                t.wait_ge(SV.h, svw)
                for k in range(KT):
                    for o in range(0, W, 512):
                        oe = min(W, o + 512)
                        t.matmul(psA[:, o:oe], xqf[:, slot, k, :],
                                 wgT[:, k, n0 + o:n0 + oe],
                                 start=(k == 0), stop=(k == KT - 1))
                        _li = t.matmul(psB[:, o:oe], xqf[:, slot, k, :],
                                       wuT[:, k, n0 + o:n0 + oe],
                                       start=(k == 0), stop=(k == KT - 1))
                _li.then_inc(TT.inc(), 1)
            on("tensor", _t)
            R[st, "tt_gu_%d" % ci] = TT.v

            def _v(v, W=W, st=st, ci=ci):
                v.wait_ge(SS.h, SS3 + 3 * st + ci + 1)
                _li = v.tensor_tensor(out=hbuf[:, 0:W], in0=hbuf[:, 0:W],
                                      in1=psB[:, 0:W], op=ALU.mult)
                _li.then_inc(SV.inc(), 1)
            on("vector", _v)
            R[st, "sv_hm_%d" % ci] = SV.v
            dwl = [(TP.h, R[st - 1, "tp_hq"])] if st >= 1 else None
            emit_quant(slot, W, hqt[:, n0:n1], src_ap=hbuf[:, 0:W],
                       lead_drain=True, dst_waits=dwl)
            R[st, "sv_v1_%d" % ci] = SV.v - 1
            R[st, "sv_v3_%d" % ci] = SV.v

    def emit_out(st):
        slot = st % 2

        def _t2(t, slot=slot, st=st):
            t.wait_ge(TP.h, R[st, "tp_hq"])
            t.wait_ge(GD.h, p3_gd)
            if st >= 1:
                t.wait_ge(SV.h, R[st - 1, "sv_out"])
            for k in range(KT):
                t.matmul(psC[:, 0:512], mxf[:, slot, k, :], woutw[:, k, 0:512],
                         start=(k == 0), stop=False)
                t.matmul(psC[:, 512:1024], mxf[:, slot, k, :],
                         woutw[:, k, 512:1024], start=(k == 0), stop=False)
            for m in range(MT):
                t.matmul(psC[:, 0:512], hq[:, m, :], wdT[:, m, 0:512],
                         start=False, stop=(m == MT - 1))
                _li = t.matmul(psC[:, 512:1024], hq[:, m, :],
                               wdT[:, m, 512:1024],
                               start=False, stop=(m == MT - 1))
            _li.then_inc(TT.inc(), 1)
        on("tensor", _t2)
        R[st, "tt_out"] = TT.v

    def emit_outadd(st):
        slot = st % 2

        def _v2(v, slot=slot, st=st):
            v.wait_ge(TT.h, R[st, "tt_out"])
            v.wait_ge(QD.h, R[st, "qd"])
            _li = v.tensor_tensor(out=xres[:, slot, :], in0=psC[:],
                                  in1=xres[:, slot, :], op=ALU.add)
            _li.then_inc(SV.inc(), 1)
        on("vector", _v2)
        R[st, "sv_out"] = SV.v

    def emit_store(st):
        slot = st % 2
        tok = st * 128

        def _sy3(sy, slot=slot, tok=tok, st=st):
            sy.wait_ge(SV.h, R[st, "sv_out"])
            if st >= 1:
                sy.wait_ge(ST.h, R[st - 1, "st_store"])
            sy.dma_start(out=out_h[tok:tok + 128, :], in_=xres[:, slot, :]
                         ).then_inc(ST.inc(16), 16)
        on("sync", _sy3)
        R[st, "st_store"] = ST.v

    for st in range(NT):
        emit_A(st)
        if st >= 1:
            emit_H(st - 1)
        emit_tile_chunks(st)
        if st >= 1:
            emit_out(st - 1)
            emit_outadd(st - 1)
            emit_store(st - 1)
    emit_H(NT - 1)
    emit_out(NT - 1)
    emit_outadd(NT - 1)
    emit_store(NT - 1)
    assert GG.v == gg_cf(NT - 1, 2)

    def _sall(s):
        for st in range(NT):
            for ci, (n0, n1) in enumerate(ngs):
                W = n1 - n0
                s.wait_ge(TT.h, R[st, "tt_gu_%d" % ci])
                if ci > 0:
                    s.wait_ge(SV.h, R[st, "sv_v1_%d" % (ci - 1)])
                    s.wait_ge(GG.h, gg_cf(st, ci - 1))
                elif st > 0:
                    s.wait_ge(SV.h, R[st - 1, "sv_v1_2"])
                    s.wait_ge(GG.h, gg_cf(st - 1, 2))
                s.activation(out=hbuf[:, 0:W], in_=psA[:, 0:W], func=AF.Silu
                             ).then_inc(SS.inc(), 1)
    on("scalar", _sall)
    assert SS.v == SS3 + 3 * NT

    st_fin = ST.v

    def _fin(sy, st_fin=st_fin):
        sy.wait_ge(ST.h, st_fin)
    on("sync", _fin)

    es3.close()
    es.close()
    return nc


_NC_CACHE = {}


def _get_nc(S):
    if S not in _NC_CACHE:
        nc = build_core_program(S)
        mybir.codegen_inst_isa_subclasses(nc)
        _NC_CACHE[S] = nc
    return _NC_CACHE[S]


def kernel(L0_attn_x, L0_attn_wq, L0_attn_wk, L0_attn_wv, L0_attn_wgq,
           L0_attn_wgk, L0_attn_wgv, L0_attn_r_out_w, L0_attn_gate_local_w,
           L0_attn_gate_remote_w, L0_attn_w_out, L0_ffn_w_gate, L0_ffn_w_up,
           L0_ffn_w_down):
    x = np.ascontiguousarray(np.asarray(L0_attn_x))
    B, S, D = x.shape
    nc = _get_nc(S)
    common = {
        "wq": np.ascontiguousarray(L0_attn_wq),
        "wk": np.ascontiguousarray(L0_attn_wk),
        "wv": np.ascontiguousarray(L0_attn_wv),
        "wgq": np.ascontiguousarray(L0_attn_wgq),
        "wgk": np.ascontiguousarray(L0_attn_wgk),
        "wgv": np.ascontiguousarray(L0_attn_wgv),
        "r_out_w": np.ascontiguousarray(L0_attn_r_out_w),
        "glw": np.ascontiguousarray(L0_attn_gate_local_w),
        "grw": np.ascontiguousarray(L0_attn_gate_remote_w),
        "w_out": np.ascontiguousarray(L0_attn_w_out),
        "w_gate": np.ascontiguousarray(L0_ffn_w_gate),
        "w_up": np.ascontiguousarray(L0_ffn_w_up),
        "w_down": np.ascontiguousarray(L0_ffn_w_down),
    }
    in_maps = [dict(common, x=x[i]) for i in range(B)]
    res = run_bass_kernel_spmd(nc, in_maps, core_ids=list(range(B)))
    out = np.stack([np.asarray(res.results[i]["out"]) for i in range(B)], axis=0)
    return out.astype(np.float32)
